# version 02 (final)
# baseline (speedup 1.0000x reference)
import math
import numpy as np
import ml_dtypes
from contextlib import ExitStack

import concourse.bass as bass
import concourse.mybir as mybir
from concourse.bass_utils import run_bass_kernel_spmd

F32 = mybir.dt.float32
BF16 = mybir.dt.bfloat16
AF = mybir.ActivationFunctionType
ALU = mybir.AluOpType
AX = mybir.AxisListType.X
BFD = ml_dtypes.bfloat16

NCORES = 8
T = 2048
D = 1024
DC = 8

ENGS = ("sync", "gpsimd", "vector", "scalar", "tensor")


class Ev:
    __slots__ = ("sem", "val", "eng")

    def __init__(self, sem, val, eng):
        self.sem, self.val, self.eng = sem, val, eng


class Prog:

    def __init__(self, nc):
        self.nc = nc
        self.items = {e: [] for e in ENGS}
        self.ctr = {}
        self.delta = {}
        self.last_wait = {e: {} for e in ENGS}
        self.state = {}

    def _ev(self, sem, inc, eng):
        v = self.ctr.get(sem, 0) + inc
        self.ctr[sem] = v
        self.delta[sem] = inc
        return Ev(sem, v, eng)

    CHAIN_ENGS = ("vector", "scalar", "gpsimd")

    def op(self, eng, fn, ev=False):
        if eng in self.CHAIN_ENGS:
            e = self._ev("s_" + eng, 1, eng)
            self.items[eng].append(("chop", fn, e))
            return e if True else None
        e = self._ev("s_" + eng, 1, eng) if ev else None
        self.items[eng].append(("op", fn, e))
        return e

    def dma(self, eng, out, in_, stream=None, **kw):
        sem = "d_" + (stream or eng)
        g = self.state.get("gate_" + sem)
        self.wait(eng, g)
        e = self._ev(sem, 16, eng)
        self.items[eng].append(("op", lambda h: h.dma_start(out=out, in_=in_, **kw), e))
        return e

    def last_ev(self, eng):
        sem = "s_" + eng
        v = self.ctr.get(sem, 0)
        return Ev(sem, v, eng) if v > 0 else None

    def gate(self, stream, ev):
        self.state["gate_d_" + stream] = ev

    def wait(self, eng, ev):
        if ev is None:
            return
        if ev.eng == eng and ev.sem.startswith("s_"):
            return
        lw = self.last_wait[eng]
        if lw.get(ev.sem, -1) >= ev.val:
            return
        lw[ev.sem] = ev.val
        self.items[eng].append(("wait", ev))

    def waits(self, eng, evs):
        best = {}
        for ev in evs:
            if ev is None or (ev.eng == eng and ev.sem.startswith("s_")):
                continue
            if ev.sem not in best or best[ev.sem].val < ev.val:
                best[ev.sem] = ev
        for ev in best.values():
            self.wait(eng, ev)

    def emit(self, es):
        sems = {n: es.enter_context(self.nc.semaphore(n)) for n in self.ctr}
        block = es.enter_context(self.nc.Block())
        deltas = self.delta

        def make_body(eng):
            def body(h):
                for item in self.items[eng]:
                    if item[0] == "wait":
                        h.wait_ge(sems[item[1].sem], item[1].val)
                    elif item[0] == "chop":
                        _, fn, e = item
                        if e.val > 1:
                            h.wait_ge(sems[e.sem], e.val - 1)
                        fn(h).then_inc(sems[e.sem], 1)
                    else:
                        _, fn, e = item
                        inst = fn(h)
                        if e is not None:
                            inst.then_inc(sems[e.sem], deltas[e.sem])
            return body

        for eng in ENGS:
            if self.items[eng]:
                getattr(block, eng)(make_body(eng))



def to_fm(x_tok_major):
    return np.ascontiguousarray(np.transpose(np.asarray(x_tok_major)))


def wswizzle(wT, nk, nm):
    K, M = wT.shape
    assert M == nm * 128 and K == nk * 128, (wT.shape, nk, nm)
    w = wT.reshape(nk, 128, nm, 128)
    return np.ascontiguousarray(np.transpose(w, (2, 1, 0, 3)).reshape(nm, 128, nk * 128))


def colsfm(v):
    v = np.asarray(v, np.float32)
    return np.ascontiguousarray(v.reshape(-1, 128).T)


ONEC = np.ones((128, 1), np.float32)
ONER = np.ones((1, 128), np.float32)




class Arena:

    def __init__(self, nc, es, name, nbytes):
        self.nc = nc
        self.back = es.enter_context(nc.sbuf_tensor(name, [128, nbytes], mybir.dt.uint8))
        self.base = nc.lookup_mloc(self.back).addr
        self.cap = nbytes
        self.off = 0

    def reset(self):
        self.off = 0

    def alloc(self, name, shape, dtype):
        per = int(np.prod(shape[1:])) * mybir.dt.size(dtype)
        h = self.nc.virtual_sbuf_tensor(name, list(shape), dtype)
        self.nc._tensor(h.name, h.shape, h.dtype, type="SB")
        mloc = self.nc.lookup_mloc(h)
        mloc.allocated = not self.nc.target_bir_lowering
        mloc.addr = self.base + self.off
        self.off = (self.off + per + 31) // 32 * 32
        assert self.off <= self.cap, (name, self.off, self.cap)
        return h.ap()


def mk_ln_ctx(p, c, es, tag, alloc=None, xb=False):
    nc = p.nc
    TTl = 512
    if alloc is None:
        def alloc(n, s, d):
            return es.enter_context(nc.sbuf_tensor(n, s, d))[:, :]
    c[tag + "_sq"] = alloc(tag + "_sq", [128, DC, TTl], BF16)
    c[tag + "_tmp"] = alloc(tag + "_tmp", [128, TTl], F32)
    c[tag + "_rows"] = alloc(tag + "_rows", [1, 3 * TTl], F32)
    if xb:
        c[tag + "_xb"] = alloc(tag + "_xb", [128, DC, TTl], BF16)


def emit_ln(p, c, get_x, put_u, tiles, src_evs, tag, gcol=None, bcol=None):
    TTl = 512
    SQ = c[tag + "_sq"]
    rows = c[tag + "_rows"]
    pmu, pm2 = c[tag + "_pmu"], c[tag + "_pm2"]
    st = p.state
    out_evs = []
    for i, w in tiles:
        xfn, ev_x = get_x(i)
        ufn, xbfn = put_u(i)
        evs_in = list(src_evs) + [ev_x]
        pbm, pbr = c[tag + "_pbc"][i % 2]
        p.waits("vector", evs_in + [st.get(tag + "_xbfree")])
        for cc in range(DC):
            ev_xb = p.op("vector", lambda h, cc=cc, xfn=xfn, xbfn=xbfn, w=w: h.tensor_copy(xbfn(cc), xfn(cc)), ev=(cc == DC - 1))
        p.waits("scalar", evs_in + [st.get(tag + "_sqfree")])
        for cc in range(DC):
            ev_sq = p.op("scalar", lambda h, cc=cc, xfn=xfn, w=w: h.activation(SQ[:, cc, 0:w], xfn(cc), AF.Square), ev=(cc == DC - 1))
        p.waits("tensor", evs_in + [ev_xb, st.get(tag + "_pstfree")])
        for cc in range(DC):
            p.op("tensor", lambda h, cc=cc, xbfn=xbfn, w=w: h.matmul(pmu[:, 0:w], c["ones_col"][:, :], xbfn(cc), start=(cc == 0), stop=(cc == DC - 1)))
        p.wait("tensor", ev_sq)
        for cc in range(DC):
            ev_st = p.op("tensor", lambda h, cc=cc, w=w: h.matmul(pm2[:, 0:w], c["ones_col"][:, :], SQ[:, cc, 0:w], start=(cc == 0), stop=(cc == DC - 1)),
                         ev=(cc == DC - 1))
        st[tag + "_sqfree"] = ev_st
        p.waits("vector", [ev_st, st.get(tag + "_rowsfree")])
        p.op("vector", lambda h, w=w: h.tensor_scalar_mul(rows[:, 0:w], pmu[:, 0:w], 1.0 / D))
        ev_pm = p.op("vector", lambda h, w=w: h.tensor_scalar_mul(rows[:, TTl:TTl + w], pm2[:, 0:w], 1.0 / D), ev=True)
        st[tag + "_pstfree"] = ev_pm
        p.op("vector", lambda h, w=w: h.tensor_mul(rows[:, 2 * TTl:2 * TTl + w], rows[:, 0:w], rows[:, 0:w]))
        p.op("vector", lambda h, w=w: h.tensor_sub(rows[:, TTl:TTl + w], rows[:, TTl:TTl + w], rows[:, 2 * TTl:2 * TTl + w]))
        ev_var = p.op("vector", lambda h, w=w: h.tensor_scalar_add(rows[:, TTl:TTl + w], rows[:, TTl:TTl + w], 1e-5), ev=True)
        p.wait("scalar", ev_var)
        p.op("scalar", lambda h, w=w: h.activation(rows[:, 2 * TTl:2 * TTl + w], rows[:, TTl:TTl + w], AF.Ln))
        ev_rstd = p.op("scalar", lambda h, w=w: h.activation(rows[:, 2 * TTl:2 * TTl + w], rows[:, 2 * TTl:2 * TTl + w], AF.Exp, scale=-0.5), ev=True)
        p.waits("tensor", [ev_rstd, st.get(tag + f"_bcfree{i % 2}")])
        p.op("tensor", lambda h, pbm=pbm, w=w: h.matmul(pbm[:, 0:w], c["ones_row_f"][:, :], rows[:, 0:w], start=True, stop=True))
        ev_bc = p.op("tensor", lambda h, pbr=pbr, w=w: h.matmul(pbr[:, 0:w], c["ones_row_f"][:, :], rows[:, 2 * TTl:2 * TTl + w], start=True, stop=True), ev=True)
        st[tag + "_rowsfree"] = ev_bc
        p.wait("vector", ev_bc)
        TMP = c[tag + "_tmp"]
        for cc in range(DC):
            p.op("vector", lambda h, cc=cc, xfn=xfn, pbm=pbm, w=w: h.tensor_sub(TMP[:, 0:w], xfn(cc), pbm[:, 0:w]))
            if gcol is None:
                evu = p.op("vector", lambda h, cc=cc, ufn=ufn, pbr=pbr, w=w: h.tensor_mul(ufn(cc), TMP[:, 0:w], pbr[:, 0:w]), ev=(cc == DC - 1))
            else:
                p.op("vector", lambda h, pbr=pbr, w=w: h.tensor_mul(TMP[:, 0:w], TMP[:, 0:w], pbr[:, 0:w]))
                evu = p.op("vector", lambda h, cc=cc, ufn=ufn, w=w: h.tensor_scalar(ufn(cc), TMP[:, 0:w], gcol[:, cc:cc + 1], bcol[:, cc:cc + 1], ALU.mult, ALU.add),
                           ev=(cc == DC - 1))
        st[tag + f"_bcfree{i % 2}"] = evu
        st[tag + "_xbfree"] = evu
        st[tag + "_xfree"] = evu
        out_evs.append(evu)
    return out_evs


def emit_matstack(p, c, pskey, wkey, rhs_fn, nk, nm, wdram, wbuf, evac, src_evs,
                  tok_tiles, wq_eng="gpsimd", wstream=None):
    ps = c[pskey]
    psfree = p.state.setdefault(pskey + "_free", [None, None])
    wgate = p.state.setdefault("wslot_gate", {})
    pi = p.state.setdefault(pskey + "_idx", [0])
    ws = wstream or "w"
    out_evs = []
    for m in range(nm):
        slot = m % 2
        p.gate(f"{ws}{slot}", wgate.get((ws, slot)))
        ev_w = p.dma(wq_eng, wbuf[:, slot, :, :].rearrange("p k n -> p (k n)"), wdram[m], stream=f"{ws}{slot}")
        p.wait("tensor", ev_w)
        p.waits("tensor", src_evs)
        ev_mm_last = None
        for ti, (off, w) in enumerate(tok_tiles):
            pslot = pi[0] % 2
            pi[0] += 1
            p.wait("tensor", psfree[pslot])
            for k in range(nk):
                ev_mm = p.op("tensor",
                             lambda h, k=k, off=off, w=w, slot=slot, pslot=pslot: h.matmul(
                                 ps[pslot][:, 0:w], wbuf[:, slot, k, :], rhs_fn(k, off, w),
                                 start=(k == 0), stop=(k == nk - 1)),
                             ev=(k == nk - 1))
            ev_mm_last = ev_mm
            e2 = evac(m, ti, ps[pslot], ev_mm)
            psfree[pslot] = e2
            out_evs.append(e2)
        wgate[(ws, slot)] = ev_mm_last
    return out_evs



def build_l1ffn():
    nc = bass.Bass("TRN2", target_bir_lowering=False)
    half = 1024
    x = nc.dram_tensor("x", [D, T], F32, kind="ExternalInput").ap()
    wab = nc.dram_tensor("wab", [64, 128, D], F32, kind="ExternalInput").ap()
    h1b = nc.dram_tensor("h1b", [32, 128, 4096], F32, kind="ExternalInput").ap()
    h2b = nc.dram_tensor("h2b", [32, 128, 4096], F32, kind="ExternalInput").ap()
    w3b = nc.dram_tensor("w3b", [8, 128, 4096], F32, kind="ExternalInput").ap()
    sab = nc.dram_tensor("sab", [128, 64], F32, kind="ExternalInput").ap()
    sh1 = nc.dram_tensor("sh1", [128, 32], F32, kind="ExternalInput").ap()
    sh2 = nc.dram_tensor("sh2", [128, 32], F32, kind="ExternalInput").ap()
    s3 = nc.dram_tensor("s3", [128, 8], F32, kind="ExternalInput").ap()
    onec = nc.dram_tensor("onec", [128, 1], F32, kind="ExternalInput").ap()
    oner = nc.dram_tensor("oner", [1, 128], F32, kind="ExternalInput").ap()
    xout = nc.dram_tensor("xout", [D, T], F32, kind="ExternalOutput").ap()
    xo = xout.rearrange("(cc p) t -> p cc t", p=128)

    es = ExitStack()
    p = Prog(nc)
    c = {}
    sb = lambda name, shape, dt=F32: es.enter_context(nc.sbuf_tensor(name, shape, dt))

    X = sb("X", [128, DC, 512])
    XR = sb("XR", [128, 2, 512])
    UB = sb("UB", [128, DC, half], BF16)
    H1 = sb("H1", [128, 32, half], BF16)
    H2 = sb("H2", [128, 32, half], BF16)
    SA = sb("SA", [128, half], F32)
    WAB = sb("WAB", [128, 2, DC, 128], BF16)
    WH = sb("WH", [128, 2, 32, 128], BF16)
    SC = sb("SC", [128, 64 + 32 + 32 + 8 + 8], F32)
    c["ones_col"] = sb("ones_col", [128, 1], BF16)
    c["ones_row_f"] = sb("ones_row_f", [1, 128], F32)
    c["ps"] = [es.enter_context(nc.psum_tensor(f"ps{i}", [128, 512], F32)) for i in range(2)]
    pstat = es.enter_context(nc.psum_tensor("pstat", [1, 1024], F32))
    pbc = [es.enter_context(nc.psum_tensor(f"pbc{i}", [128, 2, 512], F32)) for i in range(2)]
    c["ln_pmu"] = pstat[:, 0:512]
    c["ln_pm2"] = pstat[:, 512:1024]
    c["ln_pbc"] = [(pbc[i][:, 0, :], pbc[i][:, 1, :]) for i in range(2)]
    mk_ln_ctx(p, c, es, "ln")

    with es:
        p.dma("sync", SC[:, 0:64], sab, stream="c")
        p.dma("sync", SC[:, 64:96], sh1, stream="c")
        p.dma("sync", SC[:, 96:128], sh2, stream="c")
        ev_s3 = p.dma("sync", SC[:, 128:136], s3, stream="c")
        ev_oc = p.dma("gpsimd", c["ones_col"][:, :], onec, stream="cg")
        ev_or = p.dma("sync", c["ones_row_f"][:, :], oner, stream="c")
        ev_c = ev_or
        p.waits("vector", [ev_c, ev_oc])
        ev_s3h = p.op("vector", lambda h: h.tensor_scalar_mul(SC[:, 136:144], SC[:, 128:136], 0.5), ev=True)
        p.waits("scalar", [ev_c, ev_oc])
        ev_store = []

        xv = x.rearrange("(cc p) t -> p cc t", p=128)
        TILES = [(0, 512), (512, 512)]
        for hh in range(T // half):
            def get_x(i, hh=hh):
                t0 = hh * half + i * 512
                p.gate("x", p.state.get("ln_xfree"))
                ev = p.dma("sync", X[:, :, :], xv[:, :, t0:t0 + 512], stream="x")
                return (lambda cc: X[:, cc, :]), ev

            def put_u(i):
                usl = slice(i * 512, (i + 1) * 512)
                return (lambda cc: UB[:, cc, usl]), (lambda cc: UB[:, cc, usl])
            prev_a = p.state.get("prev_a_evs", [])
            evs_u = emit_ln(p, c, get_x, put_u, [(0, 512), (1, 512)], [ev_oc, ev_c] + prev_a, "ln")

            box = {"silu_ev": None, "stt_ev": None}

            def evac_a(m, tt, pst, ev_mm):
                sl = slice(tt * 512, (tt + 1) * 512)
                if m % 2 == 0:
                    p.wait("scalar", ev_mm)
                    p.wait("scalar", box["stt_ev"])
                    e = p.op("scalar", lambda h, m=m, sl=sl: h.activation(SA[:, sl], pst[:, :], AF.Silu, scale=SC[:, m:m + 1]), ev=True)
                    box["silu_ev"] = e
                else:
                    p.wait("vector", ev_mm)
                    p.wait("vector", box["silu_ev"])
                    e = p.op("vector", lambda h, m=m, sl=sl: h.scalar_tensor_tensor(
                        H1[:, m // 2, sl], pst[:, :], SC[:, m:m + 1], SA[:, sl], ALU.mult, ALU.mult), ev=True)
                    box["stt_ev"] = e
                return e

            p.waits("tensor", [ev_s3h])
            evs_a = emit_matstack(p, c, "ps", "wab", lambda k, off, w: UB[:, k, off:off + w], DC, 64, wab, WAB, evac_a, evs_u, TILES)
            p.state["prev_a_evs"] = evs_a[-2:]

            def evac_h1(m, tt, pst, ev_mm):
                sl = slice(tt * 512, (tt + 1) * 512)
                p.wait("scalar", ev_mm)
                return p.op("scalar", lambda h, m=m, sl=sl: h.activation(H2[:, m, sl], pst[:, :], AF.Gelu, scale=SC[:, 64 + m:65 + m]), ev=True)
            evs_h1 = emit_matstack(p, c, "ps", "wh", lambda k, off, w: H1[:, k, off:off + w], 32, 32, h1b, WH, evac_h1, evs_a[-2:], TILES)

            def evac_h2(m, tt, pst, ev_mm):
                sl = slice(tt * 512, (tt + 1) * 512)
                p.wait("scalar", ev_mm)
                return p.op("scalar", lambda h, m=m, sl=sl: h.activation(H1[:, m, sl], pst[:, :], AF.Gelu, scale=SC[:, 96 + m:97 + m]), ev=True)
            evs_h2 = emit_matstack(p, c, "ps", "wh", lambda k, off, w: H2[:, k, off:off + w], 32, 32, h2b, WH, evac_h2, evs_h1[-2:], TILES)

            def evac_w3(m, tt, pst, ev_mm, hh=hh):
                g0 = hh * half + tt * 512
                xslot = p.state.setdefault("xr_idx", [0])
                i = xslot[0] % 2
                xslot[0] += 1
                xrfree = p.state.setdefault("xr_free", [None, None])
                p.gate(f"xr{i}", xrfree[i])
                ev_xr = p.dma("sync", XR[:, i, :], xv[:, m, g0:g0 + 512], stream=f"xr{i}")
                p.waits("vector", [ev_mm, ev_xr])
                e = p.op("vector", lambda h, m=m, i=i: h.scalar_tensor_tensor(
                    XR[:, i, :], pst[:, :], SC[:, 136 + m:137 + m], XR[:, i, :],
                    ALU.mult, ALU.add), ev=True)
                p.wait("sync", e)
                es_ = p.dma("sync", xo[:, m, g0:g0 + 512], XR[:, i, :], stream=f"st{i}")
                ev_store.append(es_)
                xrfree[i] = es_
                return e
            emit_matstack(p, c, "ps", "wh", lambda k, off, w: H1[:, k, off:off + w], 32, 8, w3b, WH, evac_w3, evs_h2[-2:], TILES)

        p.waits("gpsimd", ev_store)
        p.emit(es)
    return nc


_PROGS = {}


def get_prog(name, builder, *a, **kw):
    if name not in _PROGS:
        _PROGS[name] = builder(*a, **kw)
    return _PROGS[name]


def l1ffn_shared_inputs(I):
    wab = np.empty((64, 128, D), np.float32)
    wab[0::2] = wswizzle(np.ascontiguousarray(np.asarray(I["L1_ffn_w1_w"]).T), DC, 32)
    wab[1::2] = wswizzle(np.ascontiguousarray(np.asarray(I["L1_ffn_w2_w"]).T), DC, 32)
    sab = np.empty((128, 64), np.float32)
    sab[:, 0::2] = colsfm(I["L1_ffn_w1_s"])
    sab[:, 1::2] = colsfm(I["L1_ffn_w2_s"])
    return dict(
        wab=wab,
        h1b=wswizzle(np.ascontiguousarray(np.asarray(I["L1_ffn_h1_w"]).T), 32, 32),
        h2b=wswizzle(np.ascontiguousarray(np.asarray(I["L1_ffn_h2_w"]).T), 32, 32),
        w3b=wswizzle(np.ascontiguousarray(np.asarray(I["L1_ffn_w3_w"]).T), 32, 8),
        sab=sab, sh1=colsfm(I["L1_ffn_h1_s"]), sh2=colsfm(I["L1_ffn_h2_s"]),
        s3=colsfm(I["L1_ffn_w3_s"]), onec=ONEC, oner=ONER)


def run_l1ffn(x_fm_full, I):
    nc = get_prog("l1ffn", build_l1ffn)
    shared = l1ffn_shared_inputs(I)
    maps = [dict(x=np.ascontiguousarray(x_fm_full[:, i * T:(i + 1) * T]), **shared)
            for i in range(NCORES)]
    res = run_bass_kernel_spmd(nc, maps, core_ids=list(range(NCORES)))
    return np.concatenate([r["xout"] for r in res.results], axis=1)





NTOTI = 2433
NGRID = 19
NQC = 17


def build_inner(with_ln=False):
    nc = bass.Bass("TRN2", target_bir_lowering=False)
    yin = nc.dram_tensor("yin", [D, NTOTI], F32, kind="ExternalInput").ap()
    wqk = nc.dram_tensor("wqk", [16, 128, D], F32, kind="ExternalInput").ap()
    wv = nc.dram_tensor("wv", [8, 128, D], F32, kind="ExternalInput").ap()
    wo = nc.dram_tensor("wo", [8, 128, D], F32, kind="ExternalInput").ap()
    w1p = nc.dram_tensor("w1p", [44, 128, D], F32, kind="ExternalInput").ap()
    w2p = nc.dram_tensor("w2p", [8, 128, 2816], F32, kind="ExternalInput").ap()
    cols = nc.dram_tensor("cols", [128, 90], F32, kind="ExternalInput").ap()
    tabs = nc.dram_tensor("tabs", [4, 128, 2432], F32, kind="ExternalInput").ap()
    mka = nc.dram_tensor("mka", [128, 128], F32, kind="ExternalInput").ap()
    mkb = nc.dram_tensor("mkb", [128, 128], F32, kind="ExternalInput").ap()
    idb = nc.dram_tensor("idb", [128, 128], F32, kind="ExternalInput").ap()
    seln = nc.dram_tensor("seln", [8, 128, 16], F32, kind="ExternalInput").ap()
    selr = nc.dram_tensor("selr", [8, 16, 128], F32, kind="ExternalInput").ap()
    onec = nc.dram_tensor("onec", [128, 1], F32, kind="ExternalInput").ap()
    oner = nc.dram_tensor("oner", [1, 128], F32, kind="ExternalInput").ap()
    yout = nc.dram_tensor("yout", [D, T], F32, kind="ExternalOutput").ap()
    yo = yout.rearrange("(cc p) t -> p cc t", p=128)
    wqki = nc.dram_tensor("wqki", [16, 128, D], BF16, kind="Internal").ap()
    woi = nc.dram_tensor("woi", [8, 128, D], BF16, kind="Internal").ap()

    CB1, CB2, CLNG, CLNB, CQS, CKS, CVC, CKB = 0, 44, 52, 60, 68, 69, 70, 71

    es = ExitStack()
    p = Prog(nc)
    c = {}
    sb = lambda name, shape, dt=F32: es.enter_context(nc.sbuf_tensor(name, shape, dt))[...]

    Y = sb("Y", [128, DC, NTOTI])
    TABF = sb("TABF", [128, 2, 2432])
    TABH = sb("TABH", [128, 2, 2432], BF16)
    WVS = sb("WVS", [128, 8, D], BF16)
    COLS = sb("COLS", [128, 90])
    MKA = sb("MKA", [128, 128], BF16)
    MKB = sb("MKB", [128, 128], BF16)
    IDB = sb("IDB", [128, 128], BF16)
    SELN = sb("SELN", [128, 8, 16], BF16)
    SELR = sb("SELR", [16, 8, 128])
    c["ones_col"] = sb("ones_col", [128, 1], BF16)
    c["ones_row_f"] = sb("ones_row_f", [1, 128], F32)
    ONEB = sb("ONEB", [1, 128], BF16)

    ar = Arena(nc, es, "arena", 76 * 1024)
    KF = ar.alloc("KF", [128, 8, 4, 128], BF16)
    VTM = ar.alloc("VTM", [128, 4, 1024], BF16)
    QF = ar.alloc("QF", [128, 8, 2, 256], BF16)
    QA = ar.alloc("QA", [128, 8, 2, 256], BF16)
    QRW = ar.alloc("QRW", [128, 8, 256], BF16)
    KRW = ar.alloc("KRW", [128, 8, 256], BF16)
    TMPA = ar.alloc("TMPA", [128, 256], F32)
    QP = ar.alloc("QP", [128, 256], F32)
    SQB = ar.alloc("SQB", [128, 2, 256], BF16)
    PTS = ar.alloc("PTS", [128, 2, 3, 128], BF16)
    AOS = ar.alloc("AOS", [65, 128], F32)
    AOB = ar.alloc("AOB", [128, 8, 256], BF16)
    RN = ar.alloc("RN", [16, 512], F32)
    BROW = ar.alloc("BROW", [1, 2, 384], BF16)
    KMX = ar.alloc("KMX", [128, 20, 8], F32)
    KMW = ar.alloc("KMW", [128, 8], F32)
    KMWB = ar.alloc("KMWB", [128, 8], BF16)
    RROW = ar.alloc("RROW", [1, 128], F32)
    YBS = ar.alloc("YBS", [128, 8, 264], BF16)
    WQB = ar.alloc("WQB", [128, 2, 8, 128], BF16)
    WOB = ar.alloc("WOB", [128, 2, 8, 128], BF16)
    attn_top = ar.off
    ar.reset()
    XN = ar.alloc("XN", [128, 8, 1024], BF16)
    WB = ar.alloc("WB", [128, 2, 22, 128], BF16)
    ffn_static_top = ar.off
    HF = ar.alloc("HF", [128, 22, 1024], BF16)
    GG = ar.alloc("GG", [128, 512], F32)
    SAF = ar.alloc("SAF", [128, 1024], BF16)
    sub_top = ar.off
    ar.off = ffn_static_top
    mk_ln_ctx(p, c, es, "ln", alloc=lambda n, s, d: ar.alloc(n, s, d), xb=True)
    assert max(ar.off, sub_top, attn_top) <= ar.cap, (ar.off, sub_top, attn_top)

    PS = [es.enter_context(nc.psum_tensor(f"pb{i}", [128, 512], F32))[...] for i in range(8)]
    c["ip_ps"] = [PS[0], PS[1]]
    c["ln_pmu"] = PS[3][0:1, 0:512]
    c["ln_pm2"] = PS[2][0:1, 0:512]
    c["ln_pbc"] = [(PS[4], PS[5]), (PS[6], PS[7])]
    st = p.state

    def yshift(cc, off, w):
        s = 1 if cc >= 4 else 0
        return Y[:, cc, off - s:off - s + w]

    def pool_slot(key="ip_ps"):
        pi = st.setdefault(key + "_idx", [0])
        fr = st.setdefault(key + "_free", [None, None])
        sl = pi[0] % 2
        pi[0] += 1
        p.wait("tensor", fr[sl])
        return sl, fr

    with es:
        ev_y = p.dma("sync", Y, yin.rearrange("(cc p) t -> p cc t", p=128), stream="y")
        for i in range(2):
            p.dma("sync", TABF[:, i, :], tabs[i], stream="c")
        for i in range(2):
            p.dma("gpsimd", TABH[:, i, :], tabs[2 + i], stream="cg")
        p.dma("sync", COLS, cols, stream="c")
        ev_or = p.dma("sync", c["ones_row_f"], oner, stream="c")
        p.dma("gpsimd", wqki, wqk, stream="cg")
        p.dma("gpsimd", woi, wo, stream="cg")
        p.dma("gpsimd", WVS, wv.rearrange("k p d -> p k d"), stream="cg")
        p.dma("gpsimd", MKA, mka, stream="cg")
        p.dma("gpsimd", MKB, mkb, stream="cg")
        p.dma("gpsimd", IDB, idb, stream="cg")
        p.dma("gpsimd", SELN, seln.rearrange("f p m -> p f m"), stream="cg")
        p.dma("gpsimd", SELR, selr.rearrange("f n t -> n f t"), stream="cg")
        p.dma("gpsimd", c["ones_col"], onec, stream="cg")
        ev_cg = p.dma("gpsimd", ONEB, oner, stream="cg")
        LOADED = [ev_y, ev_or, ev_cg]

        if with_ln:
            tls = [(0, 512), (1, 512), (2, 512), (3, 512), (4, 385)]
            offs = [0, 512, 1024, 1536, 2048]

            def get_x(i):
                return (lambda cc, i=i: Y[:, cc, offs[i]:offs[i] + tls[i][1]]), ev_y

            def put_u(i):
                return ((lambda cc, i=i: Y[:, cc, offs[i]:offs[i] + tls[i][1]]),
                        (lambda cc, i=i: c["ln_xb"][:, cc, 0:tls[i][1]]))
            evs_pro = emit_ln(p, c, get_x, put_u, tls, LOADED, "ln")
            st["attn_evs"] = list(evs_pro)
        else:
            st["attn_evs"] = []

        def ybs_copy(g):
            J = [j for j in (2 * g, 2 * g + 1) if j < NGRID]
            o = 1 + 256 * g
            W = 128 * len(J)
            p.waits("vector", LOADED + st["attn_evs"])
            p.waits("vector", [st.get("ybs_free")])
            e = None
            for cc in range(DC):
                e = p.op("vector", lambda h, cc=cc, o=o, W=W: h.tensor_copy(YBS[:, cc, 0:W + 1], Y[:, cc, o - 1:o + W]))
            st[f"ybs{g}"] = e

        def produce(g):
            J = [j for j in (2 * g, 2 * g + 1) if j < NGRID]
            o = 1 + 256 * g
            W = 128 * len(J)
            qslot = (g - 1) % 2
            ev_ybs = st[f"ybs{g}"]

            def rhs_y(k, off, w):
                s = 1 if k >= 4 else 0
                return YBS[:, k, 1 - s + off:1 - s + off + w]

            def evac_qk(m, ti, ps, ev_mm):
                isq = m < 8
                want_n = (not isq) or g > 0
                if want_n:
                    p.waits("scalar", [ev_mm, st.get(f"sqb_free{m % 2}")])
                    e_sq = p.op("scalar", lambda h, m=m, W=W: h.activation(SQB[:, m % 2, 0:W], ps[:, 0:W], AF.Square))
                    first = (m == (0 if g > 0 else 8))
                    p.waits("tensor", [e_sq] + ([st.get("nq_free")] if first else []))
                    half = 0 if isq else 256
                    e_nq = p.op("tensor", lambda h, m=m, W=W, half=half, first=first: h.matmul(
                        PS[3][0:16, half:half + W], SELN[:, m % 8, :], SQB[:, m % 2, 0:W],
                        start=first, stop=(m == 15)), ev=True)
                    st[f"sqb_free{m % 2}"] = e_nq
                    st["nq_last"] = e_nq
                dstraw = QRW if isq else KRW
                cofs = CQS if isq else CKS
                p.waits("vector", [ev_mm] + ([e_sq] if want_n else []))
                e = p.op("vector", lambda h, m=m, W=W, dstraw=dstraw, cofs=cofs: h.tensor_scalar_mul(
                    dstraw[:, m % 8, 0:W], ps[:, 0:W], COLS[:, cofs:cofs + 1]))
                return e

            p.waits("tensor", [ev_ybs] + LOADED)
            emit_matstack(p, c, "ip_ps", "wq", rhs_y, DC, 16, wqki, WQB, evac_qk,
                          [ev_ybs], [(0, W)], wq_eng="sync", wstream="wq")

            p.waits("vector", [st["nq_last"]])
            mx_evs = []
            for side in (["q"] if g > 0 else []) + ["k"]:
                rsl = slice(0, W) if side == "q" else slice(256, 256 + W)
                e_mx = p.op("vector", lambda h, rsl=rsl: h.tensor_scalar_max(RN[0:16, rsl], PS[3][0:16, rsl], 1e-24))
                mx_evs.append(e_mx)
                p.wait("scalar", e_mx)
                p.op("scalar", lambda h, rsl=rsl: h.activation(RN[0:16, rsl], RN[0:16, rsl], AF.Ln))
                e_r = p.op("scalar", lambda h, rsl=rsl: h.activation(RN[0:16, rsl], RN[0:16, rsl], AF.Exp, scale=-0.5))
                st[f"rn_{side}"] = e_r
            st["nq_free"] = mx_evs[-1]

            tsl = slice(256 * g, 256 * g + W)
            k0 = (2 * g) % 4
            for side in (["q"] if g > 0 else []) + ["k"]:
                raw = QRW if side == "q" else KRW
                TB = TABF if side == "q" else TABH
                ct, stt_ = 0, 1
                rnsl = slice(0, W) if side == "q" else slice(256, 256 + W)
                for m in range(8):
                    psl = m % 2
                    pr = PS[2][:, psl * 256:psl * 256 + W]
                    p.waits("tensor", [st.get(f"rnb_free{psl}"), st[f"rn_{side}"]])
                    e_rnb = p.op("tensor", lambda h, m=m, rnsl=rnsl, pr=pr: h.matmul(
                        pr, SELR[0:16, m, :], RN[0:16, rnsl], start=True, stop=True), ev=True)
                    p.op("vector", lambda h, m=m, W=W, raw=raw, ct=ct, tsl=tsl, TB=TB: h.tensor_mul(
                        TMPA[:, 0:W], raw[:, m, 0:W], TB[:, ct, tsl]))
                    evg = None
                    for bb in range(4):
                        p.waits("gpsimd", [st.get("qp_free"), p.last_ev("vector")] if bb == 0 else [])
                        evg = p.op("gpsimd", lambda h, m=m, bb=bb, W=W, raw=raw: h.tensor_copy(
                            QP[32 * bb:32 * bb + 32, 0:W], raw[32 * (bb ^ 1):32 * (bb ^ 1) + 32, m, 0:W]))
                    p.wait("vector", evg)
                    p.op("vector", lambda h, W=W, stt_=stt_, tsl=tsl, TB=TB: h.tensor_mul(
                        QP[:, 0:W], QP[:, 0:W], TB[:, stt_, tsl]))
                    st["qp_free"] = p.op("vector", lambda h, W=W: h.tensor_add(TMPA[:, 0:W], TMPA[:, 0:W], QP[:, 0:W]))
                    p.wait("vector", e_rnb)
                    if side == "q":
                        e4 = p.op("vector", lambda h, m=m, W=W, qslot=qslot, pr=pr: h.tensor_mul(
                            QF[:, m, qslot, 0:W], TMPA[:, 0:W], pr))
                        p.wait("scalar", e4)
                        eab = p.op("scalar", lambda h, m=m, W=W, qslot=qslot: h.activation(
                            QA[:, m, qslot, 0:W], QF[:, m, qslot, 0:W], AF.Abs))
                        st[f"qa_ready{qslot}"] = eab
                    else:
                        e4 = p.op("vector", lambda h, m=m, W=W, k0=k0, nj=len(J), pr=pr: h.tensor_mul(
                            KF[:, m, k0:k0 + nj, :].rearrange("p j t -> p (j t)"),
                            TMPA[:, 0:W], pr))
                        for ji, j in enumerate(J):
                            p.op("vector", lambda h, m=m, j=j: h.tensor_reduce(
                                KMX[:, j, m:m + 1], KF[:, m, j % 4, :], AX, ALU.max,
                                apply_absolute_value=True))
                    st[f"rnb_free{psl}"] = e4

            for ji, j in enumerate(J):
                for vh in range(2):
                    sl, fr = pool_slot()
                    e_mm = None
                    for k in range(DC):
                        e_mm = p.op("tensor", lambda h, k=k, ji=ji, vh=vh, sl=sl: h.matmul(
                            c["ip_ps"][sl][:, 0:512], rhs_y(k, 128 * ji, 128), WVS[:, k, 512 * vh:512 * vh + 512],
                            start=(k == 0), stop=(k == DC - 1)), ev=(k == DC - 1))
                    p.wait("vector", e_mm)
                    e_ev = p.op("vector", lambda h, j=j, vh=vh, sl=sl: h.tensor_copy(
                        VTM[:, j % 4, 512 * vh:512 * vh + 512], c["ip_ps"][sl][:, 0:512]))
                    fr[sl] = e_ev
            st["ybs_free"] = p.last_ev("tensor")

        def attn_pair(hp):
            qcs = [qc for qc in (2 * hp, 2 * hp + 1) if qc < NQC]
            qslot = hp % 2
            ao_evs = []
            p.waits("vector", st["attn_evs"])
            for qc in qcs:
                qpart = (qc % 2) * 128
                p.waits("vector", [st.get("kmwb_free")])
                p.op("vector", lambda h, qc=qc: h.tensor_max(KMW, KMX[:, qc, :], KMX[:, qc + 1, :]))
                p.op("vector", lambda h, qc=qc: h.tensor_max(KMW, KMW, KMX[:, qc + 2, :]))
                e_kmw = p.op("vector", lambda h: h.tensor_copy(KMWB, KMW))
                for hh in range(16):
                    hb = 64 * (hh % 2)
                    f = hh // 2
                    sb_ = hp % 2
                    stbank = PS[4 + hh % 2]
                    sl, fr = pool_slot()
                    p.waits("tensor", [e_kmw, st.get(f"qa_ready{qslot}")])
                    e_b = p.op("tensor", lambda h, hb=hb, f=f, sl=sl, qslot=qslot, qpart=qpart: h.matmul(
                        c["ip_ps"][sl][0:1, 0:128], KMWB[hb:hb + 64, f:f + 1],
                        QA[hb:hb + 64, f, qslot, qpart:qpart + 128], start=True, stop=True), ev=True)
                    p.wait("vector", e_b)
                    e_ng = None
                    for ji in range(3):
                        e_ng = p.op("vector", lambda h, ji=ji, sl=sl, hh=hh: h.tensor_scalar_mul(
                            BROW[0:1, hh % 2, 128 * ji:128 * ji + 128], c["ip_ps"][sl][0:1, 0:128], -1.0))
                    fr[sl] = e_ng
                    p.waits("tensor", [st.get(f"st_free{hh % 2}"), e_ng])
                    for ji in range(3):
                        j = qc + ji
                        p.op("tensor", lambda h, j=j, ji=ji, hb=hb, f=f, qslot=qslot, qpart=qpart, stbank=stbank: h.matmul(
                            stbank[:, 128 * ji:128 * ji + 128], KF[hb:hb + 64, f, j % 4, :],
                            QF[hb:hb + 64, f, qslot, qpart:qpart + 128], start=(ji == 0), stop=False))
                    p.op("tensor", lambda h, stbank=stbank: h.matmul(
                        stbank[:, 0:128], IDB, MKA, start=False, stop=False))
                    p.op("tensor", lambda h, stbank=stbank: h.matmul(
                        stbank[:, 256:384], IDB, MKB, start=False, stop=False))
                    e_st = p.op("tensor", lambda h, hh=hh, stbank=stbank: h.matmul(
                        stbank[:, 0:384], ONEB, BROW[0:1, hh % 2, :], start=False, stop=True), ev=True)
                    p.waits("scalar", [e_st, st.get(f"pts_free{hh % 2}")])
                    if qc <= 2:
                        for ji in range(3):
                            j = qc + ji
                            e_exp = p.op("scalar", lambda h, ji=ji, j=j, hh=hh, stbank=stbank: h.activation(
                                PTS[:, hh % 2, ji, :], stbank[:, 128 * ji:128 * ji + 128], AF.Exp,
                                scale=8.0, bias=COLS[:, CKB + j:CKB + j + 1]))
                    else:
                        e_exp = p.op("scalar", lambda h, hh=hh, stbank=stbank: h.activation(
                            PTS[:, hh % 2, :, :].rearrange("p a b -> p (a b)"), stbank[:, 0:384], AF.Exp, scale=8.0))
                    st[f"st_free{hh % 2}"] = e_exp
                    p.waits("tensor", [e_exp, st.get("pv_free"), st.get("rb_free")])
                    for ji in range(3):
                        j = qc + ji
                        p.op("tensor", lambda h, j=j, ji=ji, hh=hh: h.matmul(
                            PS[6][0:64, 0:128], VTM[:, j % 4, 64 * hh:64 * hh + 64],
                            PTS[:, hh % 2, ji, :], start=(ji == 0), stop=(ji == 2)))
                        e_pvl = p.op("tensor", lambda h, j=j, ji=ji, hh=hh: h.matmul(
                            PS[7][64:65, 0:128], c["ones_col"], PTS[:, hh % 2, ji, :],
                            start=(ji == 0), stop=(ji == 2), tile_position=(0, 64)), ev=(ji == 2))
                    p.waits("scalar", [e_pvl, st.get("aos_free")])
                    e_cp0 = p.op("scalar", lambda h: h.activation(AOS[0:64, :], PS[6][0:64, 0:128], AF.Copy))
                    e_cp = p.op("scalar", lambda h: h.activation(AOS[64:65, :], PS[7][64:65, 0:128], AF.Copy))
                    st["pv_free"] = e_cp
                    p.wait("vector", e_cp)
                    p.op("vector", lambda h: h.tensor_scalar_add(AOS[64:65, :], AOS[64:65, :], 1e-38))
                    e_rc = p.op("vector", lambda h: h.reciprocal(RROW, AOS[64:65, :]))
                    p.waits("tensor", [e_rc, st.get("rb_free")])
                    e_rb = p.op("tensor", lambda h: h.matmul(
                        PS[7][0:64, 0:128], c["ones_row_f"][:, 0:64], RROW, start=True, stop=True), ev=True)
                    p.wait("vector", e_rb)
                    e_ao = p.op("vector", lambda h, hb=hb, f=f, qpart=qpart: h.tensor_mul(
                        AOB[hb:hb + 64, f, qpart:qpart + 128], AOS[0:64, :], PS[7][0:64, 0:128]))
                    st["aos_free"] = e_ao
                    st["rb_free"] = e_ao
                    ao_evs.append(e_ao)
                st["kmwb_free"] = p.last_ev("tensor")
            qw = 128 * len(qcs)
            q0 = 257 + 128 * qcs[0]
            wo_evs = []

            def evac_wo(m, ti, ps, ev_mm):
                p.wait("vector", ev_mm)
                e = p.op("vector", lambda h, m=m, q0=q0, qw=qw: h.tensor_add(
                    Y[:, m, q0:q0 + qw], ps[:, 0:qw], Y[:, m, q0:q0 + qw]))
                wo_evs.append(e)
                return e

            emit_matstack(p, c, "ip_ps", "wo", lambda k, off, w: AOB[:, k, off:off + w],
                          DC, DC, woi, WOB, evac_wo, ao_evs, [(0, qw)], wq_eng="sync", wstream="wo")
            st["attn_evs"] = wo_evs[-2:] + [p.last_ev("scalar")]

        wg = st.setdefault("wslot_gate", {})
        for s_ in (0, 1):
            wg[("wq", s_)] = ev_cg
            wg[("wo", s_)] = ev_cg
        if with_ln:
            for e_ in ENGS:
                p.waits(e_, evs_pro)
        ybs_copy(0)
        produce(0)
        ybs_copy(1)
        produce(1)
        for hp in range(9):
            if hp + 2 <= 9:
                ybs_copy(hp + 2)
            attn_pair(hp)
            if hp + 2 <= 9:
                produce(hp + 2)

        p.waits("vector", st["attn_evs"])
        e_vc = p.op("vector", lambda h: h.tensor_scalar_mul(Y[:, :, 384:385], Y[:, :, 384:385], COLS[:, CVC:CVC + 1]))

        st["ffn_evs"] = st["attn_evs"] + [e_vc]
        for e_ in ENGS:
            p.waits(e_, st["ffn_evs"])
        wgate = st.setdefault("wslot_gate", {})
        wgate[("wf", 0)] = st["attn_evs"][0]
        wgate[("wf", 1)] = st["attn_evs"][0]
        for base in (1409, 385):
            def get_x(i, base=base):
                return (lambda cc, i=i: yshift(cc, base + 512 * i, 512)), None

            def put_u(i):
                fn = lambda cc, i=i: XN[:, cc, 512 * i:512 * i + 512]
                return fn, fn
            evs_u = emit_ln(p, c, get_x, put_u, [(0, 512), (1, 512)], LOADED + st["ffn_evs"], "ln",
                            gcol=COLS[:, CLNG:CLNG + 8], bcol=COLS[:, CLNB:CLNB + 8])

            def evac_w1(m, ti, ps, ev_mm):
                off = 512 * ti
                mh = m // 2
                bcol = COLS[:, CB1 + m:CB1 + m + 1]
                if m % 2 == 0:
                    p.wait("vector", ev_mm)
                    e = p.op("vector", lambda h, off=off, bcol=bcol: h.tensor_scalar_add(
                        SAF[:, off:off + 512], ps[:, 0:512], bcol))
                else:
                    p.waits("scalar", [ev_mm, st.get("gg_free")])
                    e_g = p.op("scalar", lambda h, bcol=bcol: h.activation(GG, ps[:, 0:512], AF.Gelu, bias=bcol))
                    p.wait("vector", e_g)
                    e = p.op("vector", lambda h, mh=mh, off=off: h.tensor_mul(
                        HF[:, mh, off:off + 512], SAF[:, off:off + 512], GG))
                    st["gg_free"] = e
                return e

            emit_matstack(p, c, "ip_ps", "wf", lambda k, off, w: XN[:, k, off:off + w],
                          DC, 44, w1p, WB[:, :, 0:8, :], evac_w1, evs_u,
                          [(0, 512), (512, 512)], wstream="wf")

            w2_evs = []

            def evac_w2(m, ti, ps, ev_mm, base=base):
                off = 512 * ti
                g0 = base + off
                o0 = base - 385 + off
                bcol = COLS[:, CB2 + m:CB2 + m + 1]
                p.wait("vector", ev_mm)
                e = p.op("vector", lambda h, m=m, g0=g0, bcol=bcol: h.scalar_tensor_tensor(
                    Y[:, m, g0:g0 + 512], ps[:, 0:512], bcol, Y[:, m, g0:g0 + 512], ALU.add, ALU.add))
                w2_evs.append(e)
                si = (m * 2 + ti) % 2
                stf = st.setdefault("stf", [None, None])
                p.gate(f"st{si}", stf[si])
                p.wait("sync", e)
                es_ = p.dma("sync", yo[:, m, o0:o0 + 512], Y[:, m, g0:g0 + 512], stream=f"st{si}")
                stf[si] = es_
                st.setdefault("store_evs", []).append(es_)
                return e

            emit_matstack(p, c, "ip_ps", "wf", lambda k, off, w: HF[:, k, off:off + w],
                          22, DC, w2p, WB, evac_w2, [], [(0, 512), (512, 512)], wstream="wf")
            st["ffn_evs"] = w2_evs[-2:]

        p.waits("gpsimd", st["store_evs"])
        p.emit(es)
    return nc




def inner_tables_np(start):
    t = np.arange(start - 384, start + 2048, dtype=np.float64)
    half = np.arange(0, 64, 2, dtype=np.float64)
    inv_freq = 1.0 / 10000.0 ** (half / 64.0)
    fr = t[:, None] * inv_freq[None, :]
    fr = np.concatenate([fr, fr], -1)
    base = (half + 0.4 * 64) / (1.4 * 64)
    power = (t - 2048.0) / 128.0
    sc = base[None, :] ** power[:, None]
    sc = np.concatenate([sc, sc], -1)
    cos, sin = np.cos(fr), np.sin(fr)
    sign = np.where(np.arange(64) % 64 < 32, -1.0, 1.0)[None, :]
    cq = (cos * sc).T
    sq = (sin * sc * sign).T
    ck = (cos / sc).T
    sk = (sin / sc * sign).T
    out = np.stack([cq, sq, ck, sk]).astype(np.float32)
    return np.concatenate([out, out], axis=1)


def inner_consts_np():
    kap = np.arange(128)[:, None]
    tau = np.arange(128)[None, :]
    mka = np.where(kap >= tau, 0.0, -30000.0).astype(np.float32)
    mkb = np.where(kap <= tau, 0.0, -30000.0).astype(np.float32)
    idb = np.eye(128, dtype=np.float32)
    seln = np.zeros((8, 128, 16), np.float32)
    selr = np.zeros((8, 16, 128), np.float32)
    for f in range(8):
        for pp in range(128):
            hh = 2 * f + (pp // 64)
            seln[f, pp, hh] = 1.0
            selr[f, hh, pp] = 1.0
    return mka, mkb, idb, seln, selr


def inner_weights_np(I, li):
    Wqkv = np.asarray(I["L0_ffn_Wqkv"][li], np.float32)
    Wo = np.asarray(I["L0_ffn_Wo"][li], np.float32)
    W1 = np.asarray(I["L0_ffn_W1"][li], np.float32)
    W2 = np.asarray(I["L0_ffn_W2"][li], np.float32)
    wq = wswizzle(np.ascontiguousarray(Wqkv[:, 0:1024]), 8, 8)
    wk = wswizzle(np.ascontiguousarray(Wqkv[:, 1024:2048]), 8, 8)
    wqk = np.concatenate([wq, wk], axis=0)
    wv = np.ascontiguousarray(Wqkv[:, 2048:3072]).reshape(8, 128, 1024)
    wo = wswizzle(Wo, 8, 8)
    a = W1[:, 0:2730]; g = W1[:, 2730:5460]
    inter = np.zeros((1024, 5632), np.float32)
    for mh in range(22):
        wa = min(128, 2730 - 128 * mh)
        inter[:, 256 * mh:256 * mh + wa] = a[:, 128 * mh:128 * mh + wa]
        inter[:, 256 * mh + 128:256 * mh + 128 + wa] = g[:, 128 * mh:128 * mh + wa]
    w1p = wswizzle(inter, 8, 44)
    w2pad = np.zeros((2816, 1024), np.float32)
    w2pad[0:2730] = W2
    w2p = wswizzle(w2pad, 22, 8)
    return wqk, wv, wo, w1p, w2p


def inner_cols_np(I, li, hcore):
    b1 = np.asarray(I["L0_ffn_b1"][li], np.float32)
    b1a = np.zeros(2816, np.float32); b1a[0:2730] = b1[0:2730]
    b1g = np.zeros(2816, np.float32); b1g[0:2730] = b1[2730:5460]
    b1i = np.zeros(5632, np.float32)
    for mh in range(22):
        b1i[256 * mh:256 * mh + 128] = b1a[128 * mh:128 * mh + 128]
        b1i[256 * mh + 128:256 * mh + 256] = b1g[128 * mh:128 * mh + 128]
    cols = np.zeros((128, 90), np.float32)
    cols[:, 0:44] = colsfm(b1i)
    cols[:, 44:52] = colsfm(np.asarray(I["L0_ffn_b2"][li], np.float32))
    cols[:, 52:60] = colsfm(np.asarray(I["L0_ffn_ln_g"][li], np.float32))
    cols[:, 60:68] = colsfm(np.asarray(I["L0_ffn_ln_b"][li], np.float32))
    cols[:, 68] = np.tile(np.asarray(I["L0_ffn_q_scale"][li], np.float32), 2)
    cols[:, 69] = np.tile(np.asarray(I["L0_ffn_k_scale"][li], np.float32), 2)
    cols[:, 70] = 0.0 if hcore == 0 else 1.0
    if hcore == 0:
        cols[:, 71 + 0] = -1e30
        cols[:, 71 + 1] = -1e30
        cols[:, 71 + 2] = -1e30
    return cols


def inner_shared_inmaps(I, li):
    wqk, wv, wo, w1p, w2p = inner_weights_np(I, li)
    mka, mkb, idb, seln, selr = inner_consts_np()
    shared = dict(wqk=wqk, wv=wv, wo=wo, w1p=w1p, w2p=w2p, mka=mka, mkb=mkb,
                  idb=idb, seln=seln, selr=selr, onec=ONEC, oner=ONER)
    per_h = []
    for h in range(2):
        per_h.append(dict(cols=inner_cols_np(I, li, h), tabs=inner_tables_np(2048 * h)))
    return shared, per_h


def inner_stage_y(y_fm_full, core):
    b, h = core // 2, core % 2
    buf = np.zeros((D, NTOTI), np.float32)
    g0 = b * 4096 + h * 2048
    lo = h * 2048 - 385
    src_lo = max(lo, 0)
    buf[:, (src_lo - lo):] = y_fm_full[:, b * 4096 + src_lo: g0 + 2048]
    return np.ascontiguousarray(buf)


def run_inner(y_fm_full, I, li, with_ln=False):
    nc = get_prog(f"inner{int(with_ln)}", build_inner, with_ln=with_ln)
    shared, per_h = inner_shared_inmaps(I, li)
    maps = []
    for core in range(NCORES):
        m = dict(yin=inner_stage_y(y_fm_full, core), **shared, **per_h[core % 2])
        maps.append(m)
    res = run_bass_kernel_spmd(nc, maps, core_ids=list(range(NCORES)))
    return np.concatenate([r["yout"] for r in res.results], axis=1)





def build_resln(alpha=0.5, energy=False):
    nc = bass.Bass("TRN2", target_bir_lowering=False)
    x = nc.dram_tensor("x", [D, T], F32, kind="ExternalInput").ap()
    y = nc.dram_tensor("y", [D, T], F32, kind="ExternalInput").ap()
    onec = nc.dram_tensor("onec", [128, 1], F32, kind="ExternalInput").ap()
    oner = nc.dram_tensor("oner", [1, 128], F32, kind="ExternalInput").ap()
    xout = nc.dram_tensor("xout", [D, T], F32, kind="ExternalOutput").ap()
    uout = nc.dram_tensor("uout", [D, T], F32, kind="ExternalOutput").ap()
    if energy:
        wq = nc.dram_tensor("wq", [8, 128, D], F32, kind="ExternalInput").ap()
        bq = nc.dram_tensor("bq", [128, 8], F32, kind="ExternalInput").ap()
        seln = nc.dram_tensor("seln", [8, 128, 16], F32, kind="ExternalInput").ap()
        epart = nc.dram_tensor("epart", [1, 1], F32, kind="ExternalOutput").ap()

    es = ExitStack()
    p = Prog(nc)
    c = {}
    sb = lambda name, shape, dt=F32: es.enter_context(nc.sbuf_tensor(name, shape, dt))[...]
    X = sb("X", [128, DC, T])
    YD = sb("YD", [128, DC, 512])
    U = sb("U", [128, DC, T])
    UB = sb("UB", [128, DC, T], BF16)
    c["ones_col"] = sb("ones_col", [128, 1], BF16)
    c["ones_row_f"] = sb("ones_row_f", [1, 128], F32)
    PS = [es.enter_context(nc.psum_tensor(f"pb{i}", [128, 512], F32))[...] for i in range(8)]
    c["ln_pmu"] = PS[3][0:1, 0:512]
    c["ln_pm2"] = PS[2][0:1, 0:512]
    c["ln_pbc"] = [(PS[4], PS[5]), (PS[6], PS[7])]
    mk_ln_ctx(p, c, es, "ln")
    if energy:
        SELN = sb("SELN", [128, 8, 16], BF16)
        BQC = sb("BQC", [128, 8])
        WQB = sb("WQB", [128, 2, 8, 128], BF16)
        SQB = sb("SQB", [128, 2, 512], BF16)
        EN = sb("EN", [16, 512])
        EAC = sb("EAC", [16, 3])
        ONEB16 = sb("ONEB16", [16, 1], BF16)

    with es:
        ev_x = p.dma("sync", X, x.rearrange("(cc p) t -> p cc t", p=128), stream="y")
        ev_oc = p.dma("gpsimd", c["ones_col"], onec, stream="cg")
        if energy:
            p.dma("gpsimd", SELN, seln.rearrange("f p m -> p f m"), stream="cg")
            p.dma("gpsimd", BQC, bq, stream="cg")
            ev_oc = p.dma("gpsimd", ONEB16, onec[0:16, :], stream="cg")
        ev_or = p.dma("sync", c["ones_row_f"], oner, stream="c")
        LOADED = [ev_x, ev_oc, ev_or]
        yv = y.rearrange("(cc p) t -> p cc t", p=128)
        st = p.state
        xo = xout.rearrange("(cc p) t -> p cc t", p=128)
        uo = uout.rearrange("(cc p) t -> p cc t", p=128)
        st_evs = []

        def get_x(i):
            t0 = i * 512
            p.gate("yd", st.get("yd_free"))
            ev_yd = p.dma("sync", YD, yv[:, :, t0:t0 + 512], stream="yd")
            p.waits("vector", LOADED + [ev_yd])
            e = None
            for cc in range(DC):
                e = p.op("vector", lambda h, cc=cc, t0=t0: h.scalar_tensor_tensor(
                    X[:, cc, t0:t0 + 512], YD[:, cc, :], alpha, X[:, cc, t0:t0 + 512],
                    ALU.mult, ALU.add))
            st["yd_free"] = e
            si = i % 2
            stf = st.setdefault("stf", [None, None])
            p.gate(f"sx{si}", stf[si])
            p.wait("sync", e)
            ev_s = p.dma("sync", xo[:, :, t0:t0 + 512], X[:, :, t0:t0 + 512], stream=f"sx{si}")
            stf[si] = ev_s
            st_evs.append(ev_s)
            return (lambda cc, t0=t0: X[:, cc, t0:t0 + 512]), e

        def put_u(i):
            t0 = i * 512
            return (lambda cc, t0=t0: U[:, cc, t0:t0 + 512]), (lambda cc, t0=t0: UB[:, cc, t0:t0 + 512])

        evs_u = emit_ln(p, c, get_x, put_u, [(i, 512) for i in range(4)], LOADED, "ln")
        ub_evs = []
        for i, ev in enumerate(evs_u):
            t0 = i * 512
            si = i % 2
            stf2 = st.setdefault("stf2", [None, None])
            p.gate(f"su{si}", stf2[si])
            p.wait("sync", ev)
            ev_s = p.dma("sync", uo[:, :, t0:t0 + 512], U[:, :, t0:t0 + 512], stream=f"su{si}")
            stf2[si] = ev_s
            st_evs.append(ev_s)
            if energy:
                p.wait("vector", ev)
                e = None
                for cc in range(DC):
                    e = p.op("vector", lambda h, cc=cc, t0=t0: h.tensor_copy(UB[:, cc, t0:t0 + 512], U[:, cc, t0:t0 + 512]))
                ub_evs.append(e)

        if energy:
            wg = st.setdefault("wslot_gate", {})
            fr = st.setdefault("rl_ps_free", [None, None])
            eac_ev = None
            for ti in range(4):
                t0 = ti * 512
                nq_last = None
                for m in range(8):
                    slot = m % 2
                    p.gate(f"w{slot}", wg.get(("w", slot)))
                    ev_w = p.dma("gpsimd", WQB[:, slot, :, :].rearrange("p k n -> p (k n)"), wq[m], stream=f"w{slot}")
                    p.waits("tensor", [ev_w, fr[slot]] + ub_evs)
                    e_mm = None
                    for k in range(DC):
                        e_mm = p.op("tensor", lambda h, k=k, slot=slot, t0=t0: h.matmul(
                            PS[slot][:, 0:512], WQB[:, slot, k, :], UB[:, k, t0:t0 + 512],
                            start=(k == 0), stop=(k == DC - 1)), ev=(k == DC - 1))
                    wg[("w", slot)] = e_mm
                    p.waits("scalar", [e_mm, st.get(f"sqb_free{slot}")])
                    e_sq = p.op("scalar", lambda h, m=m, slot=slot: h.activation(
                        SQB[:, slot, :], PS[slot][:, 0:512], AF.Square, bias=BQC[:, m:m + 1]))
                    p.wait("vector", e_sq)
                    fr[slot] = e_sq
                    p.waits("tensor", [e_sq] + ([st.get("en_free")] if m == 0 else []))
                    e_nq = p.op("tensor", lambda h, m=m, slot=slot: h.matmul(
                        PS[3][0:16, 0:512], SELN[:, m, :], SQB[:, slot, :],
                        start=(m == 0), stop=(m == 7)), ev=True)
                    st[f"sqb_free{slot}"] = e_nq
                    nq_last = e_nq
                p.wait("vector", nq_last)
                e1 = p.op("vector", lambda h: h.tensor_scalar_max(EN, PS[3][0:16, 0:512], 1e-30))
                st["en_free"] = e1
                p.wait("scalar", e1)
                p.op("scalar", lambda h: h.activation(EN, EN, AF.Ln))
                e2 = p.op("scalar", lambda h: h.activation(EN, EN, AF.Exp, scale=0.5))
                p.wait("vector", e2)
                e_red = p.op("vector", lambda h, ti=ti: h.reduce_sum(EAC[:, 2:3], EN, AX))
                if ti == 0:
                    eac_ev = p.op("vector", lambda h: h.tensor_copy(EAC[:, 0:1], EAC[:, 2:3]))
                else:
                    eac_ev = p.op("vector", lambda h: h.tensor_add(EAC[:, 0:1], EAC[:, 0:1], EAC[:, 2:3]))
            e_cast = p.op("vector", lambda h: h.tensor_copy(SQB[0:16, 0, 0:1], EAC[:, 0:1]))
            p.waits("tensor", [e_cast])
            e_fin = p.op("tensor", lambda h: h.matmul(PS[3][0:1, 0:1], ONEB16, SQB[0:16, 0, 0:1], start=True, stop=True), ev=True)
            p.wait("vector", e_fin)
            e_out = p.op("vector", lambda h: h.tensor_copy(EN[0:1, 0:1], PS[3][0:1, 0:1]))
            p.wait("sync", e_out)
            st_evs.append(p.dma("sync", epart, EN[0:1, 0:1], stream="ep"))

        p.waits("gpsimd", st_evs)
        p.emit(es)
    return nc


def run_resln(x_fm, y_fm, I, alpha, energy=False):
    nc = get_prog(f"resln{alpha}{int(energy)}", build_resln, alpha=alpha, energy=energy)
    extra = {}
    if energy:
        wqkv = np.asarray(I["L0_attn_w_qkv"], np.float32)
        wqT = np.ascontiguousarray(wqkv[0:1024, :].T)
        _, _, _, seln, _ = inner_consts_np()
        extra = dict(wq=wswizzle(wqT, 8, 8), bq=colsfm(np.asarray(I["L0_attn_b_qkv"][0:1024], np.float32)), seln=seln)
    maps = [dict(x=np.ascontiguousarray(x_fm[:, i * T:(i + 1) * T]),
                 y=np.ascontiguousarray(y_fm[:, i * T:(i + 1) * T]),
                 onec=ONEC, oner=ONER, **extra) for i in range(NCORES)]
    res = run_bass_kernel_spmd(nc, maps, core_ids=list(range(NCORES)))
    xout = np.concatenate([r["xout"] for r in res.results], axis=1)
    uout = np.concatenate([r["uout"] for r in res.results], axis=1)
    ep = np.array([r["epart"][0, 0] for r in res.results]) if energy else None
    return xout, uout, ep





def build_l0attn(phases=3):
    nc = bass.Bass("TRN2", target_bir_lowering=False)
    S = 2 * T
    uf = nc.dram_tensor("uf", [D, S], F32, kind="ExternalInput").ap()
    x = nc.dram_tensor("x", [D, T], F32, kind="ExternalInput").ap()
    ep = nc.dram_tensor("ep", [1, 8], F32, kind="ExternalInput").ap()
    wqx = nc.dram_tensor("wqx", [8, 128, D], F32, kind="ExternalInput").ap()
    wkx = nc.dram_tensor("wkx", [8, 128, D], F32, kind="ExternalInput").ap()
    wvx = nc.dram_tensor("wvx", [8, 128, D], F32, kind="ExternalInput").ap()
    wox = nc.dram_tensor("wox", [8, 128, D], F32, kind="ExternalInput").ap()
    bqc = nc.dram_tensor("bqc", [128, 8], F32, kind="ExternalInput").ap()
    bkc = nc.dram_tensor("bkc", [128, 8], F32, kind="ExternalInput").ap()
    bvr = nc.dram_tensor("bvr", [1, D], F32, kind="ExternalInput").ap()
    boc = nc.dram_tensor("boc", [128, 8], F32, kind="ExternalInput").ap()
    seln = nc.dram_tensor("seln", [8, 128, 16], F32, kind="ExternalInput").ap()
    selr = nc.dram_tensor("selr", [8, 16, 128], F32, kind="ExternalInput").ap()
    onec = nc.dram_tensor("onec", [128, 1], F32, kind="ExternalInput").ap()
    oner = nc.dram_tensor("oner", [1, 128], F32, kind="ExternalInput").ap()
    xout = nc.dram_tensor("xout", [D, T], F32, kind="ExternalOutput").ap()
    xo = xout.rearrange("(cc p) t -> p cc t", p=128)

    es = ExitStack()
    p = Prog(nc)
    c = {}
    sb = lambda name, shape, dt=F32: es.enter_context(nc.sbuf_tensor(name, shape, dt))[...]
    KF = sb("KF", [128, DC, S], BF16)
    QF = sb("QF", [128, DC, T], BF16)
    VSB = sb("VSB", [128, 32, 16, 65], BF16)
    ar = Arena(nc, es, "laarena", 25 * 1024)
    WVB = ar.alloc("WVB", [128, DC, D], BF16)
    UBF = ar.alloc("UBF", [128, DC, 512], BF16)
    a_top = ar.off
    ar.reset()
    AOB = ar.alloc("AOB", [128, DC, 512], BF16)
    XR = ar.alloc("XR", [128, 2, 512], F32)
    PTS = ar.alloc("PTS", [128, 2, 512], BF16)
    AOS = ar.alloc("AOS", [65, 512], F32)
    RROW = ar.alloc("RROW", [1, 512], F32)
    assert max(a_top, ar.off) <= ar.cap
    EN = sb("EN", [16, 512])
    EN2 = sb("EN2", [16, 512])
    NTH2 = sb("NTH2", [16, 1])
    MKR = sb("MKR", [16, 512], BF16)
    THR = sb("THR", [16, 1])
    BVRt = sb("BVRt", [1, D], BF16)
    SQB = sb("SQB", [128, 2, 512], BF16)
    COLS2 = sb("COLS2", [128, 24])
    SELN = sb("SELN", [128, 8, 16], BF16)
    SELR = sb("SELR", [16, 8, 128], BF16)
    WQB = sb("WQB", [128, 2, 8, 128], BF16)
    WOB = sb("WOB", [128, 2, 8, 128], BF16)
    ONEB = sb("ONEB", [1, 128], BF16)
    c["ones_col"] = sb("ones_col", [128, 1], BF16)
    c["ones_row_f"] = sb("ones_row_f", [1, 128], F32)
    EPS = sb("EPS", [1, 8])
    PS = [es.enter_context(nc.psum_tensor(f"pb{i}", [128, 512], F32))[...] for i in range(8)]
    c["la_ps"] = [PS[0], PS[1]]
    st = p.state

    def pool_slot(key="la_ps"):
        pi = st.setdefault(key + "_idx", [0])
        fr = st.setdefault(key + "_free", [None, None])
        sl = pi[0] % 2
        pi[0] += 1
        p.wait("tensor", fr[sl])
        return sl, fr

    with es:
        ev_ep = p.dma("sync", EPS, ep, stream="c")
        p.dma("sync", COLS2[:, 0:8], bqc, stream="c")
        p.dma("sync", COLS2[:, 8:16], bkc, stream="c")
        p.dma("sync", COLS2[:, 16:24], boc, stream="c")
        ev_or = p.dma("sync", c["ones_row_f"], oner, stream="c")
        p.dma("gpsimd", SELN, seln.rearrange("f p m -> p f m"), stream="cg")
        p.dma("gpsimd", SELR, selr.rearrange("f n t -> n f t"), stream="cg")
        p.dma("gpsimd", BVRt, bvr, stream="cg")
        p.dma("gpsimd", WVB, wvx.rearrange("k p d -> p k d"), stream="cg")
        p.dma("gpsimd", ONEB, oner, stream="cg")
        ev_oc = p.dma("gpsimd", c["ones_col"], onec, stream="cg")
        LOADED = [ev_ep, ev_or, ev_oc]
        ufv = uf.rearrange("(cc p) t -> p cc t", p=128)
        xv = x.rearrange("(cc p) t -> p cc t", p=128)
        wg = st.setdefault("wslot_gate", {})
        e_vones = p.op("vector", lambda h: h.memset(VSB[:, :, :, 64:65], 1.0))

        p.waits("vector", LOADED)
        p.op("vector", lambda h: h.reduce_sum(EPS[:, 0:1], EPS, AX))
        e_thr = p.op("vector", lambda h: h.tensor_scalar_mul(EPS[:, 0:1], EPS[:, 0:1], 0.1 / (4 * 4096 * 16)))
        p.waits("tensor", [e_thr, ev_or])
        e_tb = p.op("tensor", lambda h: h.matmul(PS[2][0:16, 0:1], c["ones_row_f"][:, 0:16], EPS[:, 0:1], start=True, stop=True), ev=True)
        p.wait("vector", e_tb)
        e_thc = p.op("vector", lambda h: h.tensor_copy(THR, PS[2][0:16, 0:1]))
        e_nth = p.op("vector", lambda h: h.tensor_scalar(NTH2, THR, THR[:, 0:1], -1.0, ALU.mult, ALU.mult))

        for sp in range(8):
            t0 = sp * 512
            p.gate("ub", st.get("ub_free"))
            ev_u = p.dma("gpsimd", UBF, ufv[:, :, t0:t0 + 512], stream="ub")

            def evac_k(m, ti, ps, ev_mm, t0=t0):
                p.wait("vector", ev_mm)
                return p.op("vector", lambda h, m=m, t0=t0: h.tensor_scalar_add(
                    KF[:, m, t0:t0 + 512], ps[:, 0:512], COLS2[:, 8 + m:9 + m]))
            evs_k = emit_matstack(p, c, "la_ps", "wk", lambda k, off, w: UBF[:, k, off:off + w],
                                  DC, DC, wkx, WQB, evac_k, [ev_u] + LOADED, [(0, 512)], wstream="wk")
            ub_last = evs_k[-1]
            e_mm = None
            for ci in range(4):
                kc = sp * 4 + ci
                for vh in range(2):
                    sl, fr = pool_slot()
                    p.waits("tensor", [ev_u, ev_oc])
                    for k in range(DC):
                        p.op("tensor", lambda h, k=k, ci=ci, vh=vh, sl=sl: h.matmul(
                            c["la_ps"][sl][:, 0:512], UBF[:, k, 128 * ci:128 * ci + 128],
                            WVB[:, k, 512 * vh:512 * vh + 512], start=(k == 0), stop=False))
                    e_mm = p.op("tensor", lambda h, vh=vh, sl=sl: h.matmul(
                        c["la_ps"][sl][:, 0:512], ONEB, BVRt[:, 512 * vh:512 * vh + 512],
                        start=False, stop=True), ev=True)
                    p.waits("vector", [e_mm, e_vones])
                    e_ev = None
                    for hh in range(8):
                        e_ev = p.op("vector", lambda h, kc=kc, vh=vh, hh=hh, sl=sl: h.tensor_copy(
                            VSB[:, kc, 8 * vh + hh, 0:64], c["la_ps"][sl][:, 64 * hh:64 * hh + 64]))
                    fr[sl] = e_ev
            st["ub_free"] = e_mm

        for ti in range(4 if phases >= 1.2 else 0):
            t0 = ti * 512
            nq_last = [None]

            def evac_q(m, tix, ps, ev_mm, t0=t0):
                slot = m % 2
                if phases < 1.5:
                    p.waits("vector", [ev_mm])
                    e_ = p.op("vector", lambda h, m=m, t0=t0: h.tensor_scalar_add(
                        QF[:, m, t0:t0 + 512], ps[:, 0:512], COLS2[:, m:m + 1]))
                    st["qstack_last"] = e_
                    return e_
                p.waits("scalar", [ev_mm, st.get(f"sqb_free{slot}")])
                e_sq = p.op("scalar", lambda h, m=m, slot=slot: h.activation(
                    SQB[:, slot, :], ps[:, 0:512], AF.Square, bias=COLS2[:, m:m + 1]))
                st[f"sqb_free{slot}"] = e_sq
                nq_last[0] = e_sq
                if phases >= 1.35:
                    p.waits("tensor", [e_sq] + ([st.get("en_free")] if m == 0 else []))
                    e_nq = p.op("tensor", lambda h, m=m, slot=slot: h.matmul(
                        PS[3][0:16, 0:512], SELN[:, m, :], SQB[:, slot, :],
                        start=(m == 0), stop=(m == 7)), ev=True)
                    st[f"sqb_free{slot}"] = e_nq
                    nq_last[0] = e_nq
                p.waits("vector", [ev_mm, e_sq])
                return p.op("vector", lambda h, m=m, t0=t0: h.tensor_scalar_add(
                    QF[:, m, t0:t0 + 512], ps[:, 0:512], COLS2[:, m:m + 1]))

            p.gate("ub", st.get("ub_free"))
            ev_uq = p.dma("gpsimd", UBF, ufv[:, :, t0:t0 + 512], stream="ub")
            st["ub_free"] = None
            emit_matstack(p, c, "la_ps", "wq", lambda k, off, w: UBF[:, k, off:off + w],
                          DC, DC, wqx, WQB, evac_q, LOADED + [ev_uq], [(0, 512)], wstream="wq")
            st["ub_free"] = st.get("sqb_free1") or st.get("qstack_last")
            if phases < 1.4:
                continue
            p.waits("vector", [nq_last[0], st.get("en_read")])
            e1 = p.op("vector", lambda h: h.tensor_scalar_max(EN, PS[3][0:16, 0:512], 1e-30))
            st["en_free"] = e1
            if phases < 1.75:
                continue
            p.waits("scalar", [e1, e_nth, st.get("en2_read")])
            e_sg = p.op("scalar", lambda h: h.activation(EN2, EN, AF.Sign, bias=NTH2[:, 0:1]))
            st["en_read"] = e_sg
            p.waits("vector", [e_sg, st.get("mkr_free")])
            e_mk = p.op("vector", lambda h: h.tensor_scalar(MKR, EN2, 1.0, 0.5, ALU.add, ALU.mult))
            st["en2_read"] = e_mk
            for f in range(DC if phases >= 2 else 0):
                sl_, fr_ = pool_slot()
                psl = sl_
                p.waits("tensor", [e_mk, st.get(f"mb_free{f % 2}")])
                if phases >= 2.008 and phases < 2.01:
                    e_mb = p.op("tensor", lambda h, f=f, psl=psl: h.matmul(
                        PS[psl][:, 0:512], SQB[0:16, 0, 0:128], SQB[0:16, 1, :], start=True, stop=True), ev=True)
                elif phases >= 2.005 and phases < 2.008:
                    e_mb = p.op("tensor", lambda h, f=f, psl=psl: h.matmul(
                        PS[psl][:, 0:512], WQB[:, 0, 0, :], UBF[:, 0, :], start=True, stop=True), ev=True)
                elif phases >= 2.02 or phases >= 3:
                    e_mb = p.op("tensor", lambda h, f=f, psl=psl: h.matmul(
                        PS[psl][:, 0:512], SELR[:, f, :], MKR, start=True, stop=True), ev=True)
                else:
                    e_mb = p.op("tensor", lambda h, f=f, psl=psl: h.matmul(
                        PS[psl][:, 0:512], SELR[:, f, :], SQB[0:16, 0, :], start=True, stop=True), ev=True)
                p.wait("vector", e_mb)
                if phases < 2.05:
                    e_qm = e_mb
                    st[f"mb_free{f % 2}"] = e_qm
                    st.setdefault("qm_evs", []).append(e_qm)
                    continue
                e_qm = p.op("vector", lambda h, f=f, t0=t0, psl=psl: h.tensor_mul(
                    QF[:, f, t0:t0 + 512], QF[:, f, t0:t0 + 512], PS[psl][:, 0:512]))
                fr_[psl] = e_qm
                st[f"mb_free{f % 2}"] = e_qm
                st.setdefault("qm_evs", []).append(e_qm)
                st["mkr_free"] = e_mb

        for e_ in ENGS:
            p.waits(e_, [st.get("ub_free")] + st.get("qm_evs", []))
        st_evs = []
        if phases < 3:
            p.waits("vector", [st.get("ub_free")] + st.get("qm_evs", []))
            e = None
            for cc in range(DC):
                e = p.op("vector", lambda h, cc=cc: h.tensor_copy(XR[:, 0, :], KF[:, cc, 0:512]))
            p.wait("sync", e)
            for cc in range(DC):
                st_evs.append(p.dma("sync", xo[:, cc, 0:512], XR[:, 0, :], stream="sx0"))
            p.waits("gpsimd", st_evs)
            p.emit(es)
            return nc
        for qb in range(4):
            q0 = qb * 512
            ao_evs = []
            for hh in range(16):
                hb = 64 * (hh % 2)
                f = hh // 2
                for kc in range(32):
                    ksl = kc % 2
                    stb = PS[4 + ksl]
                    p.waits("tensor", [st.get(f"st_free{ksl}")] + st.get("qm_evs", []))
                    for qj in range(4):
                        e_st = p.op("tensor", lambda h, hb=hb, f=f, kc=kc, q0=q0, qj=qj, stb=stb: h.matmul(
                            stb[:, 128 * qj:128 * qj + 128], KF[hb:hb + 64, f, 128 * kc:128 * kc + 128],
                            QF[hb:hb + 64, f, q0 + 128 * qj:q0 + 128 * qj + 128],
                            start=(qj == 0), stop=(qj == 3)), ev=(qj == 3))
                    if phases < 2.56:
                        st[f"st_free{ksl}"] = e_st
                        st[f"pts_free{ksl}"] = e_st
                        continue
                    p.waits("scalar", [e_st, st.get(f"pts_free{ksl}")])
                    e_exp = p.op("scalar", lambda h, ksl=ksl, stb=stb: h.activation(
                        PTS[:, ksl, :], stb[:, 0:512], AF.Exp, scale=0.125))
                    st[f"st_free{ksl}"] = e_exp
                    if phases < 2.7:
                        st[f"pts_free{ksl}"] = e_exp
                        continue
                    p.waits("tensor", [e_exp] + ([st.get("pv_free"), st.get("rb_free")] if kc == 0 else []))
                    p.op("tensor", lambda h, hh=hh, kc=kc, ksl=ksl: h.matmul(
                        PS[6][0:64, 0:512], VSB[:, kc, hh, 0:64], PTS[:, ksl, :],
                        start=(kc == 0), stop=(kc == 31)))
                    e_pv = p.op("tensor", lambda h, hh=hh, kc=kc, ksl=ksl: h.matmul(
                        PS[7][64:65, 0:512], c["ones_col"], PTS[:, ksl, :],
                        start=(kc == 0), stop=(kc == 31), tile_position=(0, 64)), ev=(kc == 31))
                    st[f"pts_free{ksl}"] = e_pv
                if phases < 2.7:
                    continue
                if phases < 2.8:
                    st["pv_free"] = e_pv
                    continue
                p.waits("scalar", [e_pv, st.get("aos_free")])
                e_cp0 = p.op("scalar", lambda h: h.activation(AOS[0:64, :], PS[6][0:64, 0:512], AF.Copy))
                e_cp = p.op("scalar", lambda h: h.activation(AOS[64:65, :], PS[7][64:65, 0:512], AF.Copy))
                st["pv_free"] = e_cp
                p.wait("vector", e_cp)
                p.op("vector", lambda h: h.tensor_scalar_add(AOS[64:65, :], AOS[64:65, :], 1e-38))
                e_rc = p.op("vector", lambda h: h.reciprocal(RROW, AOS[64:65, :]))
                p.waits("tensor", [e_rc, e_cp, st.get("rb_free")])
                e_rb = p.op("tensor", lambda h: h.matmul(
                    PS[7][0:64, 0:512], c["ones_row_f"][:, 0:64], RROW, start=True, stop=True), ev=True)
                p.wait("vector", e_rb)
                e_ao = p.op("vector", lambda h, hb=hb, f=f: h.tensor_mul(
                    AOB[hb:hb + 64, f, :], AOS[0:64, :], PS[7][0:64, 0:512]))
                st["aos_free"] = e_ao
                st["rb_free"] = e_ao
                ao_evs.append(e_ao)
            wo_evs = []

            def evac_wo(m, ti2, ps, ev_mm, q0=q0, qb=qb):
                xsl2 = (qb * 8 + m) % 2
                p.gate(f"xr{xsl2}", st.get(f"xr_free{xsl2}"))
                ev_xr = p.dma("sync", XR[:, xsl2, :], xv[:, m, q0:q0 + 512], stream=f"xr{xsl2}")
                p.waits("vector", [ev_mm, ev_xr])
                e = p.op("vector", lambda h, m=m, xsl2=xsl2: h.scalar_tensor_tensor(
                    XR[:, xsl2, :], ps[:, 0:512], COLS2[:, 16 + m:17 + m], XR[:, xsl2, :],
                    ALU.add, ALU.add))
                sti = (qb * 8 + m) % 2
                stf = st.setdefault("stf", [None, None])
                p.gate(f"sx{sti}", stf[sti])
                p.wait("sync", e)
                es_ = p.dma("sync", xo[:, m, q0:q0 + 512], XR[:, xsl2, :], stream=f"sx{sti}")
                stf[sti] = es_
                st["xr_free" + str(xsl2)] = es_
                st_evs.append(es_)
                return e

            if phases >= 2.9:
                emit_matstack(p, c, "la_ps", "wo", lambda k, off, w: AOB[:, k, off:off + w],
                              DC, DC, wox, WOB, evac_wo, ao_evs, [(0, 512)], wstream="wo")

        p.waits("gpsimd", st_evs)
        p.emit(es)
    return nc




def run_l0attn(u_fm_full, x_fm_full, eparts, I):
    nc = get_prog("l0attn", build_l0attn)
    wqkv = np.asarray(I["L0_attn_w_qkv"], np.float32)
    bqkv = np.asarray(I["L0_attn_b_qkv"], np.float32)
    _, _, _, seln, selr = inner_consts_np()
    shared = dict(
        wqx=wswizzle(np.ascontiguousarray(wqkv[0:1024].T), 8, 8),
        wkx=wswizzle(np.ascontiguousarray(wqkv[1024:2048].T), 8, 8),
        wvx=np.ascontiguousarray(wqkv[2048:3072].T).reshape(8, 128, 1024),
        wox=wswizzle(np.ascontiguousarray(np.asarray(I["L0_attn_w_out"], np.float32).T), 8, 8),
        bqc=colsfm(bqkv[0:1024]), bkc=colsfm(bqkv[1024:2048]),
        bvr=np.ascontiguousarray(bqkv[2048:3072]).reshape(1, 1024),
        boc=colsfm(np.asarray(I["L0_attn_b_out"], np.float32)),
        seln=seln, selr=selr, onec=ONEC, oner=ONER,
        ep=np.asarray(eparts, np.float32).reshape(1, 8))
    maps = []
    for core in range(NCORES):
        b, hh = core // 2, core % 2
        own = u_fm_full[:, (2 * b + hh) * T:(2 * b + hh) * T + T]
        par = u_fm_full[:, (2 * b + (1 - hh)) * T:(2 * b + (1 - hh)) * T + T]
        ufull = np.ascontiguousarray(np.concatenate([own, par], axis=1))
        maps.append(dict(uf=ufull, x=np.ascontiguousarray(x_fm_full[:, core * T:(core + 1) * T]), **shared))
    res = run_bass_kernel_spmd(nc, maps, core_ids=list(range(NCORES)))
    return np.concatenate([r["xout"] for r in res.results], axis=1)



ML = 4096
MTT = 512
MNH = 32
MP = 64
MIN_NM = 34


def build_mamba(stages=4):
    nc = bass.Bass("TRN2", target_bir_lowering=False)
    x = nc.dram_tensor("x", [D, ML], F32, kind="ExternalInput").ap()
    wip = nc.dram_tensor("wip", [MIN_NM, 128, D], F32, kind="ExternalInput").ap()
    wopj = nc.dram_tensor("wopj", [128, 16, 8, 128], F32, kind="ExternalInput").ap()
    cols = nc.dram_tensor("cols", [128, 101], F32, kind="ExternalInput").ap()
    dpc = nc.dram_tensor("dpc", [64, 32], F32, kind="ExternalInput").ap()
    dtb = nc.dram_tensor("dtb", [32, 1], F32, kind="ExternalInput").ap()
    alog = nc.dram_tensor("alog", [32, 1], F32, kind="ExternalInput").ap()
    trij = nc.dram_tensor("trij", [128, 128], F32, kind="ExternalInput").ap()
    maskn = nc.dram_tensor("maskn", [128, 128], F32, kind="ExternalInput").ap()
    idf = nc.dram_tensor("idf", [128, 128], F32, kind="ExternalInput").ap()
    onesf = nc.dram_tensor("onesf", [128, 128], F32, kind="ExternalInput").ap()
    idbd = nc.dram_tensor("idbd", [128, 128], F32, kind="ExternalInput").ap()
    onec = nc.dram_tensor("onec", [128, 1], F32, kind="ExternalInput").ap()
    oner = nc.dram_tensor("oner", [1, 128], F32, kind="ExternalInput").ap()
    yod = nc.dram_tensor("yo", [D, ML], BF16, kind="ExternalOutput").ap()
    yov = yod.rearrange("(cc p) t -> p cc t", p=128)
    xv = x.rearrange("(cc p) t -> p cc t", p=128)

    es = ExitStack()
    p = Prog(nc)
    c = {}
    st = p.state
    sb = lambda name, shape, dt=F32: es.enter_context(nc.sbuf_tensor(name, shape, dt))[...]

    WIPB = sb("WIPB", [128, 2, 4, DC, 128], BF16)
    WOP = sb("WOP", [128, 16, 8, 128], BF16)
    UB = sb("UB", [128, DC, MTT], BF16)
    ZT = sb("ZT", [128, 16, MTT], BF16)
    XF = sb("XF", [128, 16, MTT], BF16)
    BCF = sb("BCF", [128, MTT], BF16)
    DTR = sb("DTR", [32, MTT])
    DT = sb("DT", [32, MTT])
    LDT = sb("LDT", [32, MTT])
    DTA = sb("DTA", [32, MTT])
    YFM = sb("YFM", [128, 16, MTT], BF16)
    STN = sb("STN", [64, MNH * MP])
    XBCT = sb("XBCT", [128, 17, 3])
    TMPC = sb("TMPC", [128, MTT])
    BTM = sb("BTM", [128, 64], BF16)
    CTM = sb("CTM", [128, 64], BF16)
    CF0f = sb("CF0f", [64, 128])
    CF0b = sb("CF0b", [64, 128], BF16)
    GTs = sb("GTs", [128, 128], BF16)
    DTATs = sb("DTATs", [128, 32])
    CAT = sb("CAT", [128, 32])
    LDTT = sb("LDTT", [128, 32])
    E1 = sb("E1", [128, 128])
    TMP = sb("TMP", [128, 128])
    EDT = sb("EDT", [128, 128], BF16)
    MT = sb("MT", [128, 128], BF16)
    ECC = sb("ECC", [128, 1])
    E2 = sb("E2", [128, 64])
    CT2 = sb("CT2", [64, 128])
    ECS = sb("ECS", [64, 1])
    YT2 = sb("YT2", [128, 128])
    YT3 = sb("YT3", [128, 128])
    BTL = sb("BTL", [128, 64], BF16)
    DCL = sb("DCL", [128, 1])
    SPA = sb("SPA", [32, MTT])
    SPB = sb("SPB", [32, MTT])
    RWS = sb("RWS", [1, 1024])
    RSTB = sb("RSTB", [128, MTT], BF16)
    COLS = sb("COLS", [128, 101])
    DPCC = sb("DPCC", [64, 32])
    DPCB = sb("DPCB", [64, 32], BF16)
    RWCB = sb("RWCB", [128, 16], BF16)
    DTBc = sb("DTBc", [32, 1])
    ALOGc = sb("ALOGc", [32, 1])
    ACOL = sb("ACOL", [32, 1])
    TRIJ = sb("TRIJ", [128, 128])
    MASKN = sb("MASKN", [128, 128])
    IDF = sb("IDF", [128, 128])
    ONESF = sb("ONESF", [128, 128])
    IDB = sb("IDB", [128, 128], BF16)
    c["ones_col"] = sb("ones_col", [128, 1], BF16)
    c["ones_row_f"] = sb("ones_row_f", [1, 128], F32)

    ar1 = Arena(nc, es, "ar1", 16384)
    XT = ar1.alloc("XT", [128, DC, MTT], F32)
    ar1.reset()
    XTM = ar1.alloc("XTM", [128, 2048], BF16)
    ar1.reset()
    OUTB = ar1.alloc("OUTB", [128, DC, MTT], BF16)
    ar2 = Arena(nc, es, "ar2", 35072)
    XBCP = ar2.alloc("XBCP", [128, 17, 515], F32)
    ar2.reset()
    ZS = ar2.alloc("ZS", [128, 16, MTT], BF16)
    SQ = ar2.alloc("SQ", [128, 16, MTT], BF16)

    ps0 = es.enter_context(nc.psum_tensor("ps0", [128, 512], F32))[...]
    ps1 = es.enter_context(nc.psum_tensor("ps1", [128, 512], F32))[...]
    psc = es.enter_context(nc.psum_tensor("psc", [128, 512], F32))[...]
    pstat = es.enter_context(nc.psum_tensor("pstat", [1, 1024], F32))[...]
    pbc = es.enter_context(nc.psum_tensor("pbc", [128, 2, 512], F32))[...]
    c["ln_pmu"] = pstat[0:1, 0:512]
    c["ln_pm2"] = pstat[0:1, 512:1024]
    c["ln_pbc"] = [(pbc[:, 0, :], pbc[:, 1, :]), (pbc[:, 0, :], pbc[:, 1, :])]
    mk_ln_ctx(p, c, es, "ln")
    PSM = [ps0, ps1]

    with es:
        p.dma("sync", TRIJ, trij, stream="c")
        p.dma("sync", MASKN, maskn, stream="c")
        p.dma("sync", IDF, idf, stream="c")
        p.dma("sync", ONESF, onesf, stream="c")
        p.dma("sync", COLS, cols, stream="c")
        p.dma("sync", DPCC, dpc, stream="c")
        p.dma("sync", DTBc, dtb, stream="c")
        p.dma("sync", ALOGc, alog, stream="c")
        ev_c = p.dma("sync", c["ones_row_f"][:, :], oner, stream="c")
        ev_wop = p.dma("gpsimd", WOP, wopj, stream="cg")
        p.dma("gpsimd", IDB, idbd, stream="cg")
        ev_cg = p.dma("gpsimd", c["ones_col"][:, :], onec, stream="cg")
        p.waits("scalar", [ev_c, ev_cg])
        p.op("scalar", lambda h: h.activation(ACOL, ALOGc, AF.Exp))
        p.waits("vector", [ev_c, ev_cg, p.last_ev("scalar")])
        ev_ac = p.op("vector", lambda h: h.tensor_scalar_mul(ACOL, ACOL, -1.0))
        p.op("vector", lambda h: h.tensor_copy(DPCB, DPCC))
        p.op("vector", lambda h: h.tensor_copy(RWCB, COLS[:, 85:101]))
        p.op("vector", lambda h: h.memset(STN, 0.0))
        ev_init = p.op("vector", lambda h: h.memset(XBCT, 0.0))
        for hh in range(MNH):
            st[f"stn{hh}"] = ev_init

        TILES1 = [(0, 512)]
        for ti in range(8):
            t0 = ti * MTT

            def get_x(i, t0=t0):
                p.gate("x", st.get("xt_free"))
                ev = p.dma("sync", XT[:, :, :], xv[:, :, t0:t0 + MTT], stream="x")
                return (lambda cc: XT[:, cc, :]), ev

            def put_u(i):
                return (lambda cc: UB[:, cc, :]), (lambda cc: UB[:, cc, :])
            evs_u = emit_ln(p, c, get_x, put_u,
                            TILES1, [ev_c, ev_cg, ev_init, st.get("ub_mm")], "ln")

            GS = 4
            NG = (MIN_NM + GS - 1) // GS
            gevs = {}

            def load_g(g):
                if g >= NG:
                    return
                gm = min(GS, MIN_NM - g * GS)
                sl = g % 2
                p.gate(f"w{sl}", st.get(f"wrd{sl}"))
                gevs[g] = p.dma("gpsimd", WIPB[:, sl, 0:gm, :, :],
                                wip[g * GS:g * GS + gm].rearrange("m p (k j) -> p m k j", j=128),
                                stream=f"w{sl}")
            load_g(0)
            load_g(1)
            ev_mm_last = None
            for g in range(NG):
                gm = min(GS, MIN_NM - g * GS)
                sl = g % 2
                for mi in range(gm):
                    m = g * GS + mi
                    psm = PSM[m % 2]
                    p.waits("tensor", [gevs[g], st.get(f"ps_rd{m % 2}"), st.get("psd_rd"),
                                       st.get("erb_rd"), st.get("pya_rd"), st.get("pyb_rd")] + evs_u)
                    for k in range(DC):
                        e_mm = p.op("tensor", lambda h, sl=sl, mi=mi, k=k, psm=psm: h.matmul(
                            psm[:, 0:512], WIPB[:, sl, mi, k, :], UB[:, k, :],
                            start=(k == 0), stop=(k == DC - 1)), ev=(k == DC - 1))
                    ev_mm_last = e_mm
                    if mi == gm - 1:
                        st[f"wrd{sl}"] = e_mm
                    p.wait("vector", e_mm)
                    if m < 16:
                        p.wait("vector", st.get("zt_rd") if m == 0 else None)
                        e_ev = p.op("vector", lambda h, m=m, psm=psm: h.tensor_copy(ZT[:, m, :], psm[:, 0:512]))
                    elif m < 33:
                        p.wait("vector", st.get("sq_mm") if m == 16 else None)
                        e_ev = p.op("vector", lambda h, m=m, psm=psm: h.tensor_copy(
                            XBCP[:, m - 16, 3:515], psm[:, 0:512]))
                    else:
                        e_ev = p.op("vector", lambda h, psm=psm: h.tensor_copy(DTR, psm[0:32, 0:512]))
                    st[f"ps_rd{m % 2}"] = e_ev
                if g + 2 < NG:
                    load_g(g + 2)
            st["ub_mm"] = ev_mm_last

            p.wait("scalar", st[f"ps_rd{33 % 2}"])
            p.op("scalar", lambda h: h.activation(SPA, DTR, AF.Abs, bias=DTBc[:, 0:1]))
            p.op("scalar", lambda h: h.activation(SPA, SPA, AF.Exp, scale=-1.0))
            e_spa = p.op("scalar", lambda h: h.activation(SPB, DTR, AF.Relu, bias=DTBc[:, 0:1]))
            p.wait("vector", e_spa)
            e_sp1 = p.op("vector", lambda h: h.tensor_scalar_add(SPA, SPA, 1.0))
            p.wait("scalar", e_sp1)
            e_sp2 = p.op("scalar", lambda h: h.activation(SPA, SPA, AF.Ln))
            p.wait("vector", e_sp2)
            p.op("vector", lambda h: h.tensor_add(DT, SPA, SPB))
            p.wait("scalar", p.last_ev("vector"))
            ev_ldt = p.op("scalar", lambda h: h.activation(LDT, DT, AF.Ln))
            p.waits("vector", [ev_ldt, ev_ac])
            ev_dta = p.op("vector", lambda h: h.tensor_scalar_mul(DTA, DT, ACOL[:, 0:1]))

            p.op("vector", lambda h: h.tensor_copy(XBCP[:, :, 0:3], XBCT))
            conv_evs = []
            for cc in range(17):
                p.wait("vector", st.get("tmpc_rd"))
                p.op("vector", lambda h, cc=cc: h.tensor_scalar(
                    TMPC, XBCP[:, cc, 0:512], COLS[:, cc:cc + 1], None, ALU.mult))
                for k in range(1, 4):
                    e_tp = p.op("vector", lambda h, cc=cc, k=k: h.scalar_tensor_tensor(
                        TMPC, XBCP[:, cc, k:k + 512], COLS[:, k * 17 + cc:k * 17 + cc + 1],
                        TMPC, ALU.mult, ALU.add))
                p.waits("scalar", [e_tp, st.get("xf_rd") if cc == 0 else None])
                dst = (lambda h, cc=cc: h.activation(XF[:, cc, :], TMPC, AF.Silu, bias=COLS[:, 68 + cc:69 + cc])) \
                    if cc < 16 else (lambda h: h.activation(BCF, TMPC, AF.Silu, bias=COLS[:, 84:85]))
                e_cv = p.op("scalar", dst)
                st["tmpc_rd"] = e_cv
                conv_evs.append(e_cv)
            e_tail = p.op("vector", lambda h: h.tensor_copy(XBCT, XBCP[:, :, 512:515]))

            for ci in range(4 if stages >= 2 else 0):
                o = ci * 128
                tsl = [0]

                def trx(lhs_ap, rhs_ap, out_rows, out_cols, dst_fn, waits_pe, extra_v=()):
                    s = tsl[0] % 2
                    tsl[0] += 1
                    pt = PSM[s]
                    p.waits("tensor", waits_pe + [st.get(f"tp{s}"), st.get(f"ps_rd{s}")])
                    e_mm = p.op("tensor", lambda h: h.matmul(
                        pt[0:out_rows, 0:out_cols], lhs_ap, rhs_ap, start=True, stop=True), ev=True)
                    p.waits("vector", [e_mm] + list(extra_v))
                    e_ev = p.op("vector", lambda h: dst_fn(h, pt))
                    st[f"tp{s}"] = e_ev
                    st[f"ps_rd{s}"] = e_ev
                    return e_ev

                xtm_evs = []
                first_extra = [st.get("xtm_mm"), st.get("out_rd")]
                for f in range(16):
                    e = trx(XF[:, f, o:o + 128], IDB, 128, 128,
                            (lambda h, pt, f=f: h.tensor_copy(XTM[:, f * 128:f * 128 + 128], pt[0:128, 0:128])),
                            [conv_evs[f]], extra_v=(first_extra if f == 0 else ()))
                    xtm_evs.append(e)
                e_btm = trx(BCF[0:64, o:o + 128], IDB[0:64, 0:64], 128, 64,
                            (lambda h, pt: h.tensor_copy(BTM, pt[0:128, 0:64])), [conv_evs[16]])
                e_ctm = trx(BCF[64:128, o:o + 128], IDB[64:128, 64:128], 128, 64,
                            (lambda h, pt: h.tensor_copy(CTM, pt[0:128, 0:64])), [conv_evs[16]])

                def ev_cf0(h, pt):
                    h.tensor_copy(CF0f, pt[0:64, 0:128])
                    return h.tensor_copy(CF0b, pt[0:64, 0:128])
                s = tsl[0] % 2
                tsl[0] += 1
                pt = PSM[s]
                p.waits("tensor", [e_ctm, st.get(f"tp{s}"), st.get(f"ps_rd{s}")])
                e_mm = p.op("tensor", lambda h, pt=pt: h.matmul(
                    pt[0:64, 0:128], CTM, IDB, start=True, stop=True), ev=True)
                p.wait("vector", e_mm)
                p.op("vector", lambda h, pt=pt: h.tensor_copy(CF0f, pt[0:64, 0:128]))
                e_cf = p.op("vector", lambda h, pt=pt: h.tensor_copy(CF0b, pt[0:64, 0:128]))
                st[f"tp{s}"] = e_cf
                st[f"ps_rd{s}"] = e_cf
                s = tsl[0] % 2
                tsl[0] += 1
                pt = PSM[s]
                p.waits("tensor", [e_cf, conv_evs[16], st.get(f"tp{s}"), st.get(f"ps_rd{s}")])
                e_mm = p.op("tensor", lambda h, pt=pt: h.matmul(
                    pt[0:128, 0:128], BCF[0:64, o:o + 128], CF0b, start=True, stop=True), ev=True)
                p.wait("vector", e_mm)
                e_gt = p.op("vector", lambda h, pt=pt: h.tensor_copy(GTs, pt[0:128, 0:128]))
                st[f"tp{s}"] = e_gt
                st[f"ps_rd{s}"] = e_gt
                e_dtat = trx(DTA[:, o:o + 128], IDF[0:32, 0:32], 128, 32,
                             (lambda h, pt: h.tensor_copy(DTATs, pt[0:128, 0:32])), [ev_dta])
                e_cat = trx(TRIJ, DTATs, 128, 32,
                            (lambda h, pt: h.tensor_copy(CAT, pt[0:128, 0:32])), [e_dtat])
                e_ldtt = trx(LDT[:, o:o + 128], IDF[0:32, 0:32], 128, 32,
                             (lambda h, pt: h.tensor_copy(LDTT, pt[0:128, 0:32])), [ev_ldt])

                for hh in range(MNH if stages >= 3 else 0):
                    pb = 64 * (hh % 2)
                    f = hh // 2
                    p.waits("vector", [st.get("e1_rd")])
                    e_h1 = p.op("vector", lambda h, hh=hh: h.tensor_scalar(
                        E1, ONESF, CAT[:, hh:hh + 1], None, ALU.mult))
                    p.waits("tensor", [e_h1, st.get("psd_rd"), st.get("ps_rd0")])
                    e_h2 = p.op("tensor", lambda h: h.matmul(
                        ps0[:, 128:256], E1, IDF, start=True, stop=True), ev=True)
                    st["e1_rd"] = e_h2
                    p.waits("vector", [e_h2, st.get("tmp_rd")])
                    e_h3 = p.op("vector", lambda h, hh=hh: h.scalar_tensor_tensor(
                        TMP, ps0[:, 128:256], CAT[:, hh:hh + 1], MASKN, ALU.subtract, ALU.add))
                    st["psd_rd"] = e_h3
                    p.waits("scalar", [e_h3, st.get("edt_rd"), e_ldtt])
                    p.op("scalar", lambda h, hh=hh: h.activation(
                        EDT, TMP, AF.Exp, bias=LDTT[:, hh:hh + 1]))
                    e_h4 = p.op("scalar", lambda h, hh=hh: h.activation(
                        DCL, TMP[:, 127:128], AF.Exp, bias=LDTT[:, hh:hh + 1]))
                    st["tmp_rd"] = e_h4
                    last_mm3 = e_h2
                    if stages < 3.2:
                        continue
                    p.waits("vector", [e_h4, st.get("mt_rd"), e_gt])
                    e_h5 = p.op("vector", lambda h: h.tensor_mul(MT, EDT, GTs))
                    st["edt_rd"] = e_h5
                    p.waits("tensor", [e_h5, st.get("pya_rd"), st.get("ps_rd1")] + xtm_evs[f:f + 1])
                    if pb == 0:
                        e_h6 = p.op("tensor", lambda h, hh=hh: h.matmul(
                            ps1[0:64, 256:384], XTM[:, hh * 64:hh * 64 + 64], MT,
                            start=True, stop=True), ev=True)
                    else:
                        e_h6 = p.op("tensor", lambda h, hh=hh: h.matmul(
                            ps1[64:128, 256:384], XTM[:, hh * 64:hh * 64 + 64], MT,
                            start=True, stop=True, tile_position=(0, 64)), ev=True)
                    st["mt_rd"] = e_h6
                    last_mm3 = e_h6
                    if stages < 3.3:
                        continue
                    p.waits("scalar", [e_cat, st.get("ecc_rd")])
                    e_h7 = p.op("scalar", lambda h, hh=hh: h.activation(ECC, CAT[:, hh:hh + 1], AF.Exp))
                    p.waits("vector", [e_h7, st.get("e2_rd")])
                    e_h8 = p.op("vector", lambda h: h.tensor_scalar(
                        E2, ONESF[:, 0:64], ECC[:, 0:1], None, ALU.mult))
                    st["ecc_rd"] = e_h8
                    if stages < 3.25:
                        continue
                    p.waits("tensor", [e_h8, st.get("erb_rd"), st.get("ps_rd1")])
                    e_h9 = p.op("tensor", lambda h: h.matmul(
                        ps1[0:64, 128:256], E2, IDF, start=True, stop=True), ev=True)
                    st["e2_rd"] = e_h9
                    if stages < 3.27:
                        last_mm3 = e_h9
                        continue
                    p.waits("vector", [e_h9])
                    p.op("vector", lambda h: h.tensor_mul(CT2, CF0f, ps1[0:64, 128:256]))
                    e_h10 = p.op("vector", lambda h: h.tensor_copy(ECS, ps1[0:64, 255:256]))
                    st["erb_rd"] = e_h10
                    if stages < 3.29:
                        last_mm3 = e_h9
                        continue
                    p.waits("tensor", [e_h10, st.get(f"stn{hh}"), st.get("pyb_rd"), st.get("ps_rd1")])
                    if pb == 0:
                        e_h11 = p.op("tensor", lambda h, hh=hh: h.matmul(
                            ps1[0:64, 384:512], STN[:, hh * 64:hh * 64 + 64], CT2,
                            start=True, stop=True), ev=True)
                    else:
                        e_h11 = p.op("tensor", lambda h, hh=hh: h.matmul(
                            ps1[64:128, 384:512], STN[:, hh * 64:hh * 64 + 64], CT2,
                            start=True, stop=True, tile_position=(0, 64)), ev=True)
                    p.waits("vector", [e_h6, e_h11, st.get("yfm_mm")])
                    p.op("vector", lambda h, pb=pb: h.tensor_copy(
                        YT2[pb:pb + 64, :], ps1[pb:pb + 64, 256:384]))
                    e_h12 = p.op("vector", lambda h, pb=pb: h.tensor_add(
                        YT2[pb:pb + 64, :], YT2[pb:pb + 64, :], ps1[pb:pb + 64, 384:512]))
                    st["pya_rd"] = e_h12
                    st["pyb_rd"] = e_h12
                    p.op("vector", lambda h, hh=hh, pb=pb, f=f, o=o: h.tensor_scalar(
                        YT3[pb:pb + 64, 0:128], XF[pb:pb + 64, f, o:o + 128],
                        DPCC[:, hh:hh + 1], None, ALU.mult))
                    e_h13 = p.op("vector", lambda h, pb=pb, f=f, o=o: h.tensor_add(
                        YFM[pb:pb + 64, f, o:o + 128], YT3[pb:pb + 64, 0:128], YT2[pb:pb + 64, :]))
                    st.setdefault("yfm_evs", []).append(e_h13)
                    last_mm3 = e_h11
                    if stages < 3.4:
                        continue
                    p.waits("vector", [st.get("btl_rd"), e_btm])
                    e_h14 = p.op("vector", lambda h: h.tensor_scalar(
                        BTL, BTM, DCL[:, 0:1], None, ALU.mult))
                    p.waits("tensor", [e_h14, st.get("ps3_rd")] + xtm_evs[f:f + 1])
                    e_h15 = p.op("tensor", lambda h, hh=hh: h.matmul(
                        psc[0:64, 0:64], BTL, XTM[:, hh * 64:hh * 64 + 64],
                        start=True, stop=True), ev=True)
                    st["btl_rd"] = e_h15
                    p.waits("vector", [e_h15])
                    e_h16 = p.op("vector", lambda h, hh=hh: h.scalar_tensor_tensor(
                        STN[:, hh * 64:hh * 64 + 64], STN[:, hh * 64:hh * 64 + 64],
                        ECS[:, 0:1], psc[0:64, 0:64], ALU.mult, ALU.add))
                    st["ps3_rd"] = e_h16
                    st[f"stn{hh}"] = e_h16
                    last_mm3 = e_h15
                if stages >= 3:
                    st["xtm_mm"] = last_mm3

            if stages < 4:
                p.waits("vector", [e_tail, ev_dta] + conv_evs)
                e_cp = None
                for m in range(DC):
                    e_cp = p.op("vector", lambda h, m=m: h.tensor_copy(OUTB[:, m, :], XF[:, m, :]))
                p.waits("sync", [e_cp])
                p.gate("o", st.get("o_done"))
                e_st = p.dma("sync", yov[:, :, t0:t0 + MTT], OUTB[:, :, :], stream="o")
                st["o_done"] = e_st
                st["out_rd"] = e_st
                st["xt_free"] = e_st
                st.setdefault("stores", []).append(e_st)
                st.pop("yfm_evs", None)
                continue
            yfm_evs = st.pop("yfm_evs")
            p.waits("scalar", yfm_evs[-4:] + [st.get("zs_rd")])
            zs_evs = []
            for cc in range(16):
                e = p.op("scalar", lambda h, cc=cc: h.activation(ZS[:, cc, :], ZT[:, cc, :], AF.Silu))
                zs_evs.append(e)
            st["zt_rd"] = zs_evs[-1]
            p.waits("vector", yfm_evs + [zs_evs[-1]])
            yg_evs = []
            for cc in range(16):
                e = p.op("vector", lambda h, cc=cc: h.tensor_mul(YFM[:, cc, :], YFM[:, cc, :], ZS[:, cc, :]))
                yg_evs.append(e)
            st["zs_rd"] = yg_evs[-1]
            p.waits("scalar", [yg_evs[-1], st.get("sq_mm")])
            sq_evs = []
            for cc in range(16):
                e = p.op("scalar", lambda h, cc=cc: h.activation(SQ[:, cc, :], YFM[:, cc, :], AF.Square))
                sq_evs.append(e)
            p.waits("tensor", [sq_evs[-1], st.get("ln_pstfree"), st.get("pstat_rd")])
            for cc in range(16):
                e_psr = p.op("tensor", lambda h, cc=cc: h.matmul(
                    pstat[0:1, 0:512], c["ones_col"][:, :], SQ[:, cc, :],
                    start=(cc == 0), stop=(cc == 15)), ev=(cc == 15))
            st["sq_mm"] = e_psr
            p.waits("vector", [e_psr])
            e_v1 = p.op("vector", lambda h: h.tensor_scalar(
                RWS[:, 0:512], pstat[0:1, 0:512], 1.0 / 2048.0, 1e-5, ALU.mult, ALU.add))
            st["pstat_rd"] = e_v1
            st["ln_pstfree"] = e_v1
            p.wait("scalar", e_v1)
            p.op("scalar", lambda h: h.activation(RWS[:, 512:1024], RWS[:, 0:512], AF.Ln))
            e_rst = p.op("scalar", lambda h: h.activation(
                RWS[:, 0:512], RWS[:, 512:1024], AF.Exp, scale=-0.5))
            p.waits("tensor", [e_rst, st.get("ps_rd0"), st.get("psd_rd")])
            e_pbd = p.op("tensor", lambda h: h.matmul(
                ps0[:, 0:512], c["ones_row_f"][:, :], RWS[0:1, 0:512],
                start=True, stop=True), ev=True)
            p.wait("vector", e_pbd)
            e_rb = p.op("vector", lambda h: h.tensor_copy(RSTB, ps0[:, 0:512]))
            st["ps_rd0"] = e_rb
            yn_evs = []
            for cc in range(16):
                e = p.op("vector", lambda h, cc=cc: h.scalar_tensor_tensor(
                    YFM[:, cc, :], YFM[:, cc, :], COLS[:, 85 + cc:86 + cc], RSTB, ALU.mult, ALU.mult))
                yn_evs.append(e)
            st["xf_rd"] = yn_evs[-1]
            ob_evs = []
            omm_last = None
            for m in range(DC):
                psm = PSM[m % 2]
                p.waits("tensor", [ev_cg, yn_evs[-1], st.get(f"ps_rd{m % 2}"), st.get("psd_rd"),
                                   st.get("erb_rd"), st.get("pya_rd"), st.get("pyb_rd")])
                for k in range(16):
                    e_mm = p.op("tensor", lambda h, m=m, k=k, psm=psm: h.matmul(
                        psm[:, 0:512], WOP[:, k, m, :], YFM[:, k, :],
                        start=(k == 0), stop=(k == 15)), ev=(k == 15))
                omm_last = e_mm
                p.waits("vector", [e_mm, st.get("out_rd") if m == 0 else None])
                e_ev = p.op("vector", lambda h, m=m, psm=psm: h.tensor_copy(OUTB[:, m, :], psm[:, 0:512]))
                st[f"ps_rd{m % 2}"] = e_ev
                ob_evs.append(e_ev)
            st["yfm_mm"] = omm_last
            p.waits("sync", ob_evs)
            p.gate("o", st.get("o_done"))
            e_st = p.dma("sync", yov[:, :, t0:t0 + MTT], OUTB[:, :, :], stream="o")
            st["o_done"] = e_st
            st["out_rd"] = e_st
            st["xt_free"] = e_st
            st.setdefault("stores", []).append(e_st)

        p.waits("gpsimd", st["stores"])
        p.emit(es)
    return nc


def mamba_dir_inputs(I, pfx):
    W = np.asarray(I[f"L1_attn_{pfx}_in_proj_w"], np.float32)
    Wpad = np.zeros((4352, 1024), np.float32)
    Wpad[0:4256] = W
    convw = np.asarray(I[f"L1_attn_{pfx}_conv_w"], np.float32)[:, 0, :]
    convb = np.asarray(I[f"L1_attn_{pfx}_conv_b"], np.float32)
    rmsw = np.asarray(I[f"L1_attn_{pfx}_rms_w"], np.float32)
    cols = np.zeros((128, 101), np.float32)
    cols[:, 0:68] = convw.reshape(4, 17, 128).transpose(2, 0, 1).reshape(128, 68)
    cols[:, 68:85] = convb.reshape(17, 128).T
    cols[:, 85:101] = rmsw.reshape(16, 128).T
    dpc = np.repeat(np.asarray(I[f"L1_attn_{pfx}_D"], np.float32)[None, :], 64, 0)
    return dict(
        wip=wswizzle(np.ascontiguousarray(Wpad.T), DC, MIN_NM),
        wopj=np.ascontiguousarray(
            wswizzle(np.ascontiguousarray(np.asarray(I[f"L1_attn_{pfx}_out_proj_w"], np.float32).T), 16, 8)
            .reshape(8, 128, 16, 128).transpose(1, 2, 0, 3)),
        cols=cols, dpc=np.ascontiguousarray(dpc),
        dtb=np.asarray(I[f"L1_attn_{pfx}_dt_bias"], np.float32).reshape(32, 1),
        alog=np.asarray(I[f"L1_attn_{pfx}_A_log"], np.float32).reshape(32, 1))


def mamba_consts():
    one = np.ones((128, 128), np.float32)
    return dict(trij=np.ascontiguousarray(np.triu(one)),
                maskn=np.ascontiguousarray(np.tril(one, -1) * -1e30),
                idf=np.eye(128, dtype=np.float32),
                onesf=one, idbd=np.eye(128, dtype=np.float32),
                onec=ONEC, oner=ONER)


def run_mamba(x4_np, I):
    nc = get_prog("mamba", build_mamba)
    cst = mamba_consts()
    sh_f = mamba_dir_inputs(I, "f")
    sh_b = mamba_dir_inputs(I, "b")
    maps = []
    for core in range(NCORES):
        b = core % 4
        xb = np.asarray(x4_np[b], np.float32)
        if core >= 4:
            xb = xb[::-1]
        maps.append(dict(x=to_fm(xb), **(sh_f if core < 4 else sh_b), **cst))
    res = run_bass_kernel_spmd(nc, maps, core_ids=list(range(NCORES)))
    yf = np.stack([np.asarray(res.results[b]["yo"]) for b in range(4)])
    yb = np.stack([np.asarray(res.results[4 + b]["yo"])[:, ::-1] for b in range(4)])
    return yf, yb



def build_mcombine():
    nc = bass.Bass("TRN2", target_bir_lowering=False)
    x = nc.dram_tensor("x", [D, T], F32, kind="ExternalInput").ap()
    yc = nc.dram_tensor("yc", [2048, T], BF16, kind="ExternalInput").ap()
    wpj = nc.dram_tensor("wpj", [128, 16, 8, 128], F32, kind="ExternalInput").ap()
    colsx = nc.dram_tensor("colsx", [128, 24], F32, kind="ExternalInput").ap()
    onec = nc.dram_tensor("onec", [128, 1], F32, kind="ExternalInput").ap()
    oner = nc.dram_tensor("oner", [1, 128], F32, kind="ExternalInput").ap()
    xout = nc.dram_tensor("xout", [D, T], F32, kind="ExternalOutput").ap()
    xo = xout.rearrange("(cc p) t -> p cc t", p=128)
    xv = x.rearrange("(cc p) t -> p cc t", p=128)
    ycv = yc.rearrange("(cc p) t -> p cc t", p=128)

    es = ExitStack()
    p = Prog(nc)
    c = {}
    st = p.state
    sb = lambda name, shape, dt=F32: es.enter_context(nc.sbuf_tensor(name, shape, dt))[...]
    WPJ = sb("WPJ", [128, 16, 8, 128], BF16)
    XT = sb("XT", [128, DC, 512])
    YC = sb("YC", [128, 16, 512], BF16)
    UF = sb("UF", [128, DC, 512])
    HT = sb("HT", [128, DC, 512])
    COLSX = sb("COLSX", [128, 24])
    c["ones_col"] = sb("ones_col", [128, 1], BF16)
    c["ones_row_f"] = sb("ones_row_f", [1, 128], F32)
    ps = [es.enter_context(nc.psum_tensor(f"ps{i}", [128, 512], F32))[...] for i in range(2)]
    pstat = es.enter_context(nc.psum_tensor("pstat", [1, 1024], F32))[...]
    pbc = [es.enter_context(nc.psum_tensor(f"pbc{i}", [128, 2, 512], F32))[...] for i in range(2)]
    c["ln_pmu"] = pstat[0:1, 0:512]
    c["ln_pm2"] = pstat[0:1, 512:1024]
    c["ln_pbc"] = [(pbc[i][:, 0, :], pbc[i][:, 1, :]) for i in range(2)]
    mk_ln_ctx(p, c, es, "ln", xb=True)
    XB = c["ln_xb"]

    with es:
        p.dma("sync", COLSX, colsx, stream="c")
        ev_c = p.dma("sync", c["ones_row_f"][:, :], oner, stream="c")
        p.dma("gpsimd", WPJ, wpj, stream="cg")
        ev_cg = p.dma("gpsimd", c["ones_col"][:, :], onec, stream="cg")

        for ti in range(4):
            t0 = ti * 512

            def get_x(i, t0=t0):
                p.gate("x", st.get("xt_free"))
                ev = p.dma("sync", XT[:, :, :], xv[:, :, t0:t0 + 512], stream="x")
                return (lambda cc: XT[:, cc, :]), ev

            def put_u(i):
                return (lambda cc: UF[:, cc, :]), (lambda cc: XB[:, cc, :])
            evs_u = emit_ln(p, c, get_x, put_u, [(0, 512)],
                            [ev_c, ev_cg, st.get("u_rd")], "ln")
            p.gate("y", st.get("yc_mm"))
            ev_y = p.dma("sync", YC[:, :, :], ycv[:, :, t0:t0 + 512], stream="y")
            ht_evs = []
            e_mm = None
            for m in range(DC):
                psm = ps[m % 2]
                p.waits("tensor", [ev_y, ev_cg, st.get(f"ps_rd{m % 2}")] + evs_u)
                for k in range(16):
                    e_mm = p.op("tensor", lambda h, m=m, k=k, psm=psm: h.matmul(
                        psm[:, 0:512], WPJ[:, k, m, :], YC[:, k, :],
                        start=(k == 0), stop=(k == 15)), ev=(k == 15))
                p.waits("vector", [e_mm] + evs_u)
                e_ev = p.op("vector", lambda h, m=m, psm=psm: h.scalar_tensor_tensor(
                    HT[:, m, :], psm[:, 0:512], COLSX[:, m:m + 1], UF[:, m, :],
                    ALU.add, ALU.add))
                st[f"ps_rd{m % 2}"] = e_ev
                ht_evs.append(e_ev)
            st["yc_mm"] = e_mm

            def get_h(i):
                return (lambda cc: HT[:, cc, :]), ht_evs[-1]

            def put_v(i):
                return (lambda cc: UF[:, cc, :]), (lambda cc: XB[:, cc, :])
            evs_v = emit_ln(p, c, get_h, put_v, [(0, 512)], ht_evs,
                            "ln", gcol=COLSX[:, 8:16], bcol=COLSX[:, 16:24])
            st["u_rd"] = None
            p.waits("vector", evs_v)
            out_evs = []
            for cc in range(DC):
                e_o = p.op("vector", lambda h, cc=cc: h.tensor_add(
                    HT[:, cc, :], XT[:, cc, :], UF[:, cc, :]))
                out_evs.append(e_o)
            st["u_rd"] = out_evs[-1]
            p.waits("sync", out_evs)
            p.gate("o", st.get("o_done"))
            e_st = p.dma("sync", xo[:, :, t0:t0 + 512], HT[:, :, :], stream="o")
            st["o_done"] = e_st
            st["xt_free"] = e_st
            st.setdefault("stores", []).append(e_st)

        p.waits("gpsimd", st["stores"])
        p.emit(es)
    return nc


def run_mcombine(x4_fm, ycat_fm, I):
    nc = get_prog("mcombine", build_mcombine)
    colsx = np.zeros((128, 24), np.float32)
    colsx[:, 0:8] = colsfm(I["L1_attn_proj_b"])
    colsx[:, 8:16] = colsfm(I["L1_attn_ln_w"])
    colsx[:, 16:24] = colsfm(I["L1_attn_ln_b"])
    wpj = np.ascontiguousarray(
        wswizzle(np.ascontiguousarray(np.asarray(I["L1_attn_proj_w"], np.float32).T), 16, 8)
        .reshape(8, 128, 16, 128).transpose(1, 2, 0, 3))
    maps = []
    for core in range(NCORES):
        sl = slice(core * T, (core + 1) * T)
        maps.append(dict(x=np.ascontiguousarray(x4_fm[:, sl]),
                         yc=np.ascontiguousarray(ycat_fm[:, sl]),
                         wpj=wpj, colsx=colsx, onec=ONEC, oner=ONER))
    res = run_bass_kernel_spmd(nc, maps, core_ids=list(range(NCORES)))
    return np.concatenate([r["xout"] for r in res.results], axis=1)


def kernel(**inputs):
    I = inputs
    x0 = np.asarray(I["L0_attn_x"], np.float32)
    x0fm = to_fm(x0.reshape(-1, D))

    y = run_inner(x0fm, I, 0, with_ln=True)
    for li in range(1, 6):
        y = run_inner(y, I, li)
    x1, u1, eparts = run_resln(x0fm, y, I, 0.5, energy=True)
    x2 = run_l0attn(u1, x1, eparts, I)
    y = run_inner(x2, I, 0, with_ln=True)
    for li in range(1, 6):
        y = run_inner(y, I, li)
    x3, _, _ = run_resln(x2, y, I, 0.5, energy=False)
    x4fm = run_l1ffn(x3, I)
    x4tok = np.ascontiguousarray(x4fm.T).reshape(4, 4096, D)
    yf, yb = run_mamba(x4tok, I)
    ycat = np.concatenate([np.concatenate(list(yf), axis=1),
                           np.concatenate(list(yb), axis=1)], axis=0)
    x5fm = run_mcombine(x4fm, ycat, I)
    xofm = run_l1ffn(x5fm, I)
    return np.ascontiguousarray(xofm.T).reshape(4, 4096, D).astype(np.float32)
